# version 03
# baseline (speedup 1.0000x reference)
import math
import itertools
import numpy as np
import ml_dtypes
from contextlib import ExitStack

import concourse.bass as bass
import concourse.mybir as mybir
from concourse.bass_utils import run_bass_kernel_spmd

F32 = mybir.dt.float32
BF16 = mybir.dt.bfloat16
I32 = mybir.dt.int32
AF = mybir.ActivationFunctionType
ALU = mybir.AluOpType
AX = mybir.AxisListType
BF = ml_dtypes.bfloat16

S = 2048
D = 4096
DC = 32
HD = 128
NH = 32
EPS = 1e-5
INVSQ = 1.0 / math.sqrt(128.0)

ENGS = ("sync", "vector", "tensor", "scalar", "gpsimd")
SEMS = ("wA", "wB", "rA", "rB", "sA", "sB", "qA", "qB",
        "gA", "gB", "gC", "gD", "pe", "dv", "ac", "gp", "cc")

_uid = itertools.count()


class P:
    def __init__(self, nc, es):
        self.nc = nc
        self.ops = {e: [] for e in ENGS}
        self.sems = {}
        self.cnt = {}
        for s in SEMS:
            self.sems[s] = es.enter_context(nc.semaphore(s))
            self.cnt[s] = 0

    def wait(self, eng, sem, val):
        if val is None or val <= 0:
            return
        s = self.sems[sem]
        self.ops[eng].append(lambda e, s=s, v=val: e.wait_ge(s, v))

    def dma(self, eng, out, in_, sem):
        s = self.sems[sem]
        self.cnt[sem] += 16
        self.ops[eng].append(
            lambda e, o=out, i=in_, s=s: e.dma_start(out=o, in_=i).then_inc(s, 16))
        return self.cnt[sem]

    def op(self, eng, fn, sem=None):
        if sem is None:
            self.ops[eng].append(fn)
            return None
        s = self.sems[sem]
        self.cnt[sem] += 1
        self.ops[eng].append(lambda e, f=fn, s=s: f(e).then_inc(s, 1))
        return self.cnt[sem]

    def barrier(self):
        snap = dict(self.cnt)
        for eng in ENGS:
            for sem, v in snap.items():
                if v > 0:
                    self.wait(eng, sem, v)

    def emit(self):
        with self.nc.Block() as block:
            for eng in ENGS:
                def run(e, L=self.ops[eng]):
                    for f in L:
                        f(e)
                getattr(block, eng)(run)


def _pt(nc, name, shape, dt=F32):
    return nc.psum_tensor(f"{name}_{next(_uid)}", shape, dt)


def _mk(a_bytes, b_bytes):
    nc = bass.Bass("TRN2", target_bir_lowering=False)
    es = ExitStack()
    p = P(nc, es)
    V = {}
    if a_bytes:
        A = es.enter_context(nc.sbuf_tensor(f"AA_{next(_uid)}",
                                            [128, a_bytes], mybir.dt.uint8))
        V["Abf"] = A[:].bitcast(BF16)
        V["Af32"] = A[:].bitcast(F32)
    if b_bytes:
        Bt = es.enter_context(nc.sbuf_tensor(f"BB_{next(_uid)}",
                                             [128, b_bytes], mybir.dt.uint8))
        V["Bbf"] = Bt[:].bitcast(BF16)
        V["Bf32"] = Bt[:].bitcast(F32)
        V["Bi32"] = Bt[:].bitcast(I32)
    return nc, es, p, V


def ph_load_A(p, Abf, srcs):
    for c, src in enumerate(srcs):
        p.dma("sync", Abf[:, c * S:(c + 1) * S], src, "rA")


def ph_load_x(p, V, x_fm, xbf):
    Abf, Bf32 = V["Abf"], V["Bf32"]
    stage = Bf32[:, 0:2048]
    cv = 0
    for c in range(DC):
        p.wait("sync", "ac", cv)
        wv = p.dma("sync", stage, x_fm[c * 128:(c + 1) * 128, :], "wA")
        p.wait("scalar", "wA", wv)
        dst = Abf[:, c * S:(c + 1) * S]
        cv = p.op("scalar", lambda e, s=stage, d=dst: e.copy(out=d, in_=s), "ac")
        p.wait("gpsimd", "ac", cv)
        p.dma("gpsimd", xbf[c], dst, "sA")


def ph_proj(p, nc, V, w_hbm, KC, OC, rhs_chunk, dest, scale=None):
    Bbf, Bf32 = V["Bbf"], V["Bf32"]
    ph = ExitStack()
    ps = ph.enter_context(_pt(nc, "pj", [128, 2048], F32))
    stage = Bf32[:, 0:KC * 128]
    wbuf = Bbf[:, 8192:8192 + KC * 128]
    slab = Bbf[:, 12288:14336]
    cv = 0
    ev = 0
    sv = 0
    for oc in range(OC):
        p.wait("sync", "ac", cv)
        wv = p.dma("sync", stage, w_hbm[oc], "wA")
        p.wait("scalar", "wA", wv)
        p.wait("scalar", "sA", sv)
        cv = p.op("scalar", lambda e, s=stage, w=wbuf: e.copy(out=w, in_=s), "ac")
        p.wait("tensor", "ac", cv)
        mmv = None
        for tt in range(4):
            for kc in range(KC):
                def mm(e, o=ps[:, tt * 512:(tt + 1) * 512],
                       w=wbuf[:, kc * 128:(kc + 1) * 128],
                       r=rhs_chunk(kc, slice(tt * 512, tt * 512 + 512)),
                       st=(kc == 0), sp=(kc == KC - 1)):
                    return e.matmul(o, w, r, start=st, stop=sp,
                                    skip_group_check=True)
                last = (tt == 3) and (kc == KC - 1)
                mmv = p.op("tensor", mm, "pe" if last else None)
        p.wait("scalar", "pe", mmv)
        if scale is None:
            ev = p.op("scalar", lambda e, s=ps[:], d=slab: e.copy(
                out=d, in_=s), "ac")
        else:
            ev = p.op("scalar", lambda e, s=ps[:], d=slab: e.mul(
                out=d, in_=s, mul=scale), "ac")
        p.wait("gpsimd", "ac", ev)
        sv = p.dma("gpsimd", dest[oc], slab, "sA")
        p.wait("tensor", "ac", ev)
    ph.close()


def ph_attention(p, nc, V, layer, heads, dest, sm):
    Abf, Bbf, Bf32 = V["Abf"], V["Bbf"], V["Bf32"]
    ph = ExitStack()
    psS = ph.enter_context(_pt(nc, "psS", [128, 2048], F32))
    psPV = ph.enter_context(_pt(nc, "psPV", [128, 512], F32))
    psZ = ph.enter_context(_pt(nc, "psZ", [1, 512], F32))
    psT = ph.enter_context(_pt(nc, "psT", [128, 256], BF16))
    psR = ph.enter_context(_pt(nc, "psR", [1, 512], F32))

    o = {}
    off = 0
    for nm, n in [("P1q", S), ("P2q", S), ("P1k", S), ("P2k", S),
                  ("Vt", S), ("QROT", S), ("KROT", S), ("SQq", S),
                  ("SQk", S), ("vT", S), ("pT", S),
                  ("outA", 512), ("outB", 512)]:
        o[nm] = Bbf[:, off:off + n]
        off += n
    cD = Abf[:, 0:2048]
    sD = Abf[:, 2048:4096]
    masks = Abf[:, 4096:6144]
    Ccol = Bf32[:, 12100:12101]
    sc1 = Bf32[0:1, 12102:12103]
    sc2 = Bf32[0:1, 12104:12105]
    kt = Bf32[0:1, 12106:12107]
    urow_f = Bf32[0:1, 12544:13056]
    uBf = Bf32[:, 13056:13568]
    cs_h = sm["cs_h"]
    urow_h = sm["urow_h"]
    ones_c, ones_r, negones_r = sm["ones_c"], sm["ones_r"], sm["negones_r"]
    ident = sm["ident"]

    out_free = [0, 0]
    n_out = 0
    prev_kv = -1
    for (h, qsrc, ksrc, vsrc, ochunk, kvid, causal) in heads:
        new_kv = (kvid != prev_kv)
        prev_kv = kvid
        p.barrier()
        p.dma("sync", o["P1q"], qsrc, "rA")
        p.dma("sync", o["P2q"][0:64, :], qsrc[64:128, :], "rA")
        lv = p.dma("sync", o["P2q"][64:128, :], qsrc[0:64, :], "rA")
        if new_kv:
            p.dma("sync", o["P1k"], ksrc, "rA")
            p.dma("sync", o["P2k"][0:64, :], ksrc[64:128, :], "rA")
            p.dma("sync", o["P2k"][64:128, :], ksrc[0:64, :], "rA")
            lv = p.dma("sync", o["Vt"], vsrc, "rA")
        p.wait("vector", "rA", lv)

        def rot(dst, p1, p2, tmp):
            p.op("vector", lambda e, dst=dst, p1=p1: e.tensor_mul(
                out=dst, in0=p1, in1=cD), "dv")
            mv = p.op("vector", lambda e, tmp=tmp, p2=p2: e.tensor_mul(
                out=tmp, in0=p2, in1=sD), "dv")
            p.wait("vector", "dv", mv)
            return p.op("vector", lambda e, dst=dst, tmp=tmp: e.tensor_add(
                out=dst, in0=dst, in1=tmp), "dv")

        rq = rot(o["QROT"], o["P1q"], o["P2q"], o["SQq"])
        if layer == 1:
            p.wait("vector", "dv", rq)
            rq = p.op("vector", lambda e: e.tensor_scalar(
                out=o["QROT"], in0=o["QROT"], scalar1=INVSQ, scalar2=None,
                op0=ALU.mult), "dv")
        if new_kv:
            rot(o["KROT"], o["P1k"], o["P2k"], o["SQk"])
        rot_done = p.cnt["dv"]

        p.wait("scalar", "dv", rot_done)
        p.op("scalar", lambda e: e.activation(out=o["SQq"], in_=o["QROT"],
                                              func=AF.Square), "ac")
        if new_kv:
            p.op("scalar", lambda e: e.activation(out=o["SQk"], in_=o["KROT"],
                                                  func=AF.Square), "ac")
        sq_done = p.cnt["ac"]
        p.wait("tensor", "ac", sq_done)

        def colmax(src, dst):
            for tt in range(4):
                mv = p.op("tensor", lambda e, tt=tt: e.matmul(
                    psR[:], ones_c[:], src[:, tt * 512:(tt + 1) * 512],
                    start=True, stop=True, skip_group_check=True), "pe")
                p.wait("vector", "pe", mv)
                if tt == 0:
                    rv = p.op("vector", lambda e: e.reduce_max(
                        out=dst, in_=psR[:], axis=AX.X), "dv")
                else:
                    rv = p.op("vector", lambda e: e.reduce_max(
                        out=kt[:], in_=psR[:], axis=AX.X), "dv")
                    p.wait("vector", "dv", rv)
                    rv = p.op("vector", lambda e: e.tensor_max(
                        out=dst, in0=dst, in1=kt[:]), "dv")
                p.wait("tensor", "dv", rv)
            return rv

        colmax(o["SQq"], sc2)
        if new_kv:
            colmax(o["SQk"], sc1)
        mx_done = p.cnt["dv"]
        p.wait("vector", "dv", mx_done)
        cv = p.op("vector", lambda e: e.tensor_tensor(
            out=sc2, in0=sc2, in1=sc1, op=ALU.mult), "dv")
        p.wait("scalar", "dv", cv)
        cv = p.op("scalar", lambda e: e.activation(
            out=cs_h[:], in_=sc2, func=AF.Sqrt), "ac")
        p.wait("tensor", "ac", cv)
        cv = p.op("tensor", lambda e: e.matmul(
            psPV[:, 0:1], negones_r[:], cs_h[:], start=True, stop=True,
            skip_group_check=True), "pe")
        p.wait("vector", "pe", cv)
        cv = p.op("vector", lambda e: e.tensor_copy(
            out=Ccol, in_=psPV[:, 0:1]), "dv")
        p.wait("scalar", "dv", cv)
        ccol_done = cv

        if new_kv:
            tv = 0
            for tc in range(16):
                tsl = tc % 2
                p.wait("tensor", "ac", tv)
                mv = p.op("tensor", lambda e, tc=tc, tsl=tsl: e.transpose(
                    psT[:, tsl * 128:(tsl + 1) * 128],
                    o["Vt"][:, tc * 128:(tc + 1) * 128], ident[:]), "pe")
                p.wait("scalar", "pe", mv)
                tv = p.op("scalar", lambda e, tc=tc, tsl=tsl: e.copy(
                    out=o["vT"][:, tc * 128:(tc + 1) * 128],
                    in_=psT[:, tsl * 128:(tsl + 1) * 128]), "ac")
            p.wait("tensor", "ac", tv)

        psS_free = {}
        pT_free = {}
        pv_rel = 0
        for st_ in range(4):
            ssl = slice(st_ * 512, (st_ + 1) * 512)
            tcs = list(range(4 * (st_ + 1))) if causal else list(range(16))
            for i, tc in enumerate(tcs):
                sb = tc % 4
                ssb = slice(sb * 512, (sb + 1) * 512)
                p.wait("tensor", "ac", psS_free.get(sb, 0))
                smm = p.op("tensor", lambda e, tc=tc, ssb=ssb, ssl=ssl:
                           e.matmul(psS[:, ssb],
                                    o["KROT"][:, tc * 128:(tc + 1) * 128],
                                    o["QROT"][:, ssl], start=True, stop=True,
                                    skip_group_check=True), "pe")
                p.wait("scalar", "pe", smm)
                p.wait("scalar", "pe", pT_free.get(sb, 0))
                ev = p.op("scalar", lambda e, ssb=ssb: e.activation(
                    out=o["pT"][:, ssb], in_=psS[:, ssb], func=AF.Exp,
                    bias=Ccol, scale=1.0), "ac")
                psS_free[sb] = ev
                dep = ("ac", ev)
                if causal:
                    delta = tc * 128 - st_ * 512
                    if 0 <= delta <= 384:
                        mi = delta // 128
                        p.wait("vector", "ac", ev)
                        mv = p.op("vector", lambda e, ssb=ssb, mi=mi:
                                  e.tensor_tensor(
                                      out=o["pT"][:, ssb],
                                      in0=o["pT"][:, ssb],
                                      in1=masks[:, mi * 512:(mi + 1) * 512],
                                      op=ALU.mult), "dv")
                        dep = ("dv", mv)
                p.wait("tensor", dep[0], dep[1])
                if i == 0:
                    p.wait("tensor", "dv", pv_rel)
                first = (i == 0)
                lastc = (i == len(tcs) - 1)
                p.op("tensor", lambda e, ssb=ssb, first=first, lastc=lastc:
                     e.matmul(psZ[:], ones_c[:], o["pT"][:, ssb],
                              start=first, stop=lastc,
                              skip_group_check=True), None)
                pT_free[sb] = p.op(
                    "tensor", lambda e, ssb=ssb, tc=tc, first=first,
                    lastc=lastc: e.matmul(
                        psPV[:], o["vT"][:, tc * 128:(tc + 1) * 128],
                        o["pT"][:, ssb], start=first, stop=lastc,
                        skip_group_check=True), "pe")
            pv_done = pT_free[tcs[-1] % 4]
            p.wait("vector", "pe", pv_done)
            v5 = p.op("vector", lambda e: e.reciprocal(out=urow_f,
                                                       in_=psZ[:]), "dv")
            p.wait("vector", "dv", v5)
            v6 = p.op("vector", lambda e: e.tensor_copy(out=urow_h[:],
                                                        in_=urow_f), "dv")
            p.wait("tensor", "dv", v6)
            p.wait("tensor", "ac", psS_free.get(0, 0))
            ub = p.op("tensor", lambda e: e.matmul(
                psS[:, 0:512], ones_r[:], urow_h[:], start=True, stop=True,
                skip_group_check=True), "pe")
            p.wait("scalar", "pe", ub)
            p.wait("scalar", "dv", pv_rel)
            ub = p.op("scalar", lambda e: e.copy(out=uBf,
                                                 in_=psS[:, 0:512]), "ac")
            psS_free[0] = ub
            osl = n_out % 2
            n_out += 1
            osem = "sA" if osl == 0 else "sB"
            obuf = o["outA"] if osl == 0 else o["outB"]
            p.wait("vector", "ac", ub)
            p.wait("vector", osem, out_free[osl])
            v8 = p.op("vector", lambda e, obuf=obuf: e.tensor_tensor(
                out=obuf, in0=psPV[:], in1=uBf, op=ALU.mult), "dv")
            pv_rel = v8
            p.wait("gpsimd", "dv", v8)
            out_free[osl] = p.dma(
                "gpsimd", dest[ochunk][:, ssl], obuf, osem)
    ph.close()


def ph_resid_ln(p, nc, V, w_hbm, KC, rhs_chunk, xold, out, out_f32, sm,
                bias_cols=None, wbf16=False, resid_scr=None):
    Abf, Bbf, Bf32 = V["Abf"], V["Bbf"], V["Bf32"]
    OC = DC
    ph = ExitStack()
    psW = ph.enter_context(_pt(nc, "psW", [128, 1024], F32))
    psSum = ph.enter_context(_pt(nc, "psSum", [1, 1024], F32))
    psSq = ph.enter_context(_pt(nc, "psSq", [1, 1024], F32))
    stage = Bf32[:, 0:KC * 128]
    wbuf = Bbf[:, 8192:8192 + KC * 128]
    xslab = Bbf[:, 12288:13312]
    rslab = Bbf[:, 13312:14336]
    sqslab = Bbf[:, 14336:15360]
    meanB = Bbf[:, 15360:16384]
    rstdB = Bbf[:, 16384:17408]
    nslab = Bbf[:, 17408:18432]
    oslab_b = Bbf[:, 18432:19456]
    oslab_f = Bf32[:, 9728:10752]
    lnrows = sm["lnrows"]
    lnrow_h = sm["lnrow_h"]
    ones_c, ones_r = sm["ones_c"], sm["ones_r"]

    for th in range(2):
        tw = slice(th * 1024, (th + 1) * 1024)
        cv = 0
        rv = 0
        sv = 0
        st_pe = 0
        xv = 0
        sqv = 0
        for oc in range(OC):
            if not wbf16:
                p.wait("sync", "ac", cv)
                wv = p.dma("sync", stage, w_hbm[oc], "wA")
                p.wait("scalar", "wA", wv)
                p.wait("scalar", "pe", st_pe)
                cv = p.op("scalar", lambda e, s=stage, w=wbuf: e.copy(
                    out=w, in_=s), "ac")
                p.wait("tensor", "ac", cv)
            else:
                p.wait("sync", "pe", st_pe)
                wv = p.dma("sync", wbuf, w_hbm[oc], "wA")
                p.wait("tensor", "wA", wv)
            if xold is not None:
                p.wait("gpsimd", "dv", xv)
                xv16 = p.dma("gpsimd", xslab, xold[oc][:, tw], "gA")
            p.wait("tensor", "dv", rv)
            mmv = None
            for tt in range(2):
                for kc in range(KC):
                    def mm(e, o=psW[:, tt * 512:(tt + 1) * 512],
                           w=wbuf[:, kc * 128:(kc + 1) * 128],
                           r=rhs_chunk(kc, slice(th * 1024 + tt * 512,
                                                 th * 1024 + tt * 512 + 512)),
                           st=(kc == 0), sp=(kc == KC - 1)):
                        return e.matmul(o, w, r, start=st, stop=sp,
                                        skip_group_check=True)
                    last = (tt == 1) and (kc == KC - 1)
                    mmv = p.op("tensor", mm, "pe" if last else None)
            p.wait("vector", "pe", mmv)
            p.wait("vector", "pe", st_pe)
            p.wait("vector", "ac", sqv)
            p.wait("vector", "sA", sv)
            if xold is not None:
                p.wait("vector", "gA", xv16)
                xv = None
                xin = xslab
            else:
                xin = rhs_chunk(oc, tw) if KC == DC else None
                assert xin is not None
            if bias_cols is not None:
                rv = p.op("vector", lambda e, sps=psW, xin=xin, oc=oc,
                          d=rslab: e.scalar_tensor_tensor(
                              out=d, in0=sps[:],
                              scalar=bias_cols[:, oc:oc + 1],
                              in1=xin, op0=ALU.add, op1=ALU.add), "dv")
            else:
                rv = p.op("vector", lambda e, sps=psW, xin=xin,
                          d=rslab: e.tensor_add(out=d, in0=sps[:],
                                                in1=xin), "dv")
            if xold is not None:
                xv = rv
            p.wait("gpsimd", "dv", rv)
            sv = p.dma("gpsimd", resid_scr[oc][:, tw], rslab, "sA")
            p.wait("scalar", "dv", rv)
            p.wait("scalar", "pe", st_pe)
            sqv = p.op("scalar", lambda e, s=rslab, d=sqslab:
                       e.activation(out=d, in_=s, func=AF.Square), "ac")
            p.wait("tensor", "dv", rv)
            p.wait("tensor", "ac", sqv)
            for half in range(2):
                hs = slice(half * 512, (half + 1) * 512)
                p.op("tensor", lambda e, hs=hs, s=rslab, oc=oc:
                     e.matmul(psSum[:, hs], ones_c[:], s[:, hs],
                              start=(oc == 0), stop=(oc == OC - 1),
                              skip_group_check=True), None)
                st_pe = p.op(
                    "tensor", lambda e, hs=hs, s=sqslab, oc=oc:
                    e.matmul(psSq[:, hs], ones_c[:], s[:, hs],
                             start=(oc == 0), stop=(oc == OC - 1),
                             skip_group_check=True),
                    "pe" if half == 1 else None)
        p.wait("vector", "pe", st_pe)
        p.wait("scalar", "pe", st_pe)
        mh = p.op("vector", lambda e, th=th: e.tensor_scalar(
            out=lnrow_h[0:1, th * 2048:th * 2048 + 1024], in0=psSum[:],
            scalar1=1.0 / D, scalar2=None, op0=ALU.mult), "dv")
        v1 = p.op("vector", lambda e: e.tensor_scalar(
            out=lnrows[0:1, 0:1024], in0=psSq[:], scalar1=1.0 / D,
            scalar2=None, op0=ALU.mult), "dv")
        p.wait("scalar", "dv", mh)
        v2 = p.op("scalar", lambda e: e.activation(
            out=psSum[:], in_=psSum[:], func=AF.Square, scale=1.0 / D), "ac")
        p.wait("vector", "ac", v2)
        p.wait("vector", "dv", v1)
        v3 = p.op("vector", lambda e: e.scalar_tensor_tensor(
            out=lnrows[0:1, 0:1024], in0=lnrows[0:1, 0:1024], scalar=EPS,
            in1=psSum[:], op0=ALU.add, op1=ALU.subtract), "dv")
        p.wait("scalar", "dv", v3)
        v4 = p.op("scalar", lambda e: e.activation(
            out=psSum[:], in_=lnrows[0:1, 0:1024], func=AF.Sqrt), "ac")
        p.wait("vector", "ac", v4)
        v5 = p.op("vector", lambda e: e.reciprocal(
            out=lnrows[0:1, 0:1024], in_=psSum[:]), "dv")
        p.wait("vector", "dv", v5)
        p.op("vector", lambda e, th=th: e.tensor_copy(
            out=lnrow_h[0:1, th * 2048 + 1024:th * 2048 + 2048],
            in_=lnrows[0:1, 0:1024]), "dv")
        p.barrier()
    for th in range(2):
        tw = slice(th * 1024, (th + 1) * 1024)
        for which, dstB in [(0, meanB), (1, rstdB)]:
            base = th * 2048 + which * 1024
            p.op("tensor", lambda e, base=base: e.matmul(
                psW[:, 0:512], ones_r[:],
                lnrow_h[0:1, base:base + 512], start=True, stop=True,
                skip_group_check=True), None)
            bv = p.op("tensor", lambda e, base=base: e.matmul(
                psW[:, 512:1024], ones_r[:],
                lnrow_h[0:1, base + 512:base + 1024], start=True, stop=True,
                skip_group_check=True), "pe")
            p.wait("scalar", "pe", bv)
            bv = p.op("scalar", lambda e, dstB=dstB: e.copy(
                out=dstB, in_=psW[:]), "ac")
            p.wait("tensor", "ac", bv)
        p.wait("vector", "ac", p.cnt["ac"])
        nv_rel = 0
        st_rel = 0
        for oc in range(DC):
            p.wait("gpsimd", "dv", nv_rel)
            nv = p.dma("gpsimd", nslab, resid_scr[oc][:, tw], "gC")
            p.wait("vector", "gC", nv)
            p.wait("vector", "sB", st_rel)
            osl = oslab_f if out_f32 else oslab_b
            vv = p.op("vector", lambda e, osl=osl: e.tensor_tensor(
                out=osl, in0=nslab, in1=meanB, op=ALU.subtract), "dv")
            nv_rel = vv
            p.wait("vector", "dv", vv)
            vv = p.op("vector", lambda e, osl=osl: e.tensor_tensor(
                out=osl, in0=osl, in1=rstdB, op=ALU.mult), "dv")
            p.wait("gpsimd", "dv", vv)
            if out_f32:
                st_rel = p.dma("gpsimd", out[oc * 128:(oc + 1) * 128, tw],
                               oslab_f, "sB")
            else:
                st_rel = p.dma("gpsimd", out[oc][:, tw], oslab_b, "sB")
        p.barrier()
    ph.close()


def ph_swiglu(p, nc, V, v1_b, b1s, p_chunks, act_scr):
    Abf, Bbf, Bf32 = V["Abf"], V["Bbf"], V["Bf32"]
    ph = ExitStack()
    psG = ph.enter_context(_pt(nc, "psG", [128, 2048], F32))
    psU = ph.enter_context(_pt(nc, "psU", [128, 2048], F32))
    PB = Abf[:, 0:16384]
    stage = Bf32[:, 0:1024]
    wbuf = Bbf[:, 2048:3072]
    silu_t = Bbf[:, 3072:5120]
    actsl = Bbf[:, 5120:7168]
    for kc in range(8):
        p.dma("sync", PB[:, kc * 2048:(kc + 1) * 2048], p_chunks[kc], "rA")
    p.barrier()
    cv = 0
    gate_rel = 0
    up_rel = 0
    silu_rel = 0
    act_st = 0
    for i in range(86):
        done = {}
        for gu in range(2):
            oc = i if gu == 0 else 86 + i
            p.wait("sync", "ac", cv)
            wv = p.dma("sync", stage, v1_b[oc], "wA")
            p.wait("scalar", "wA", wv)
            cv = p.op("scalar", lambda e, s=stage, w=wbuf: e.copy(
                out=w, in_=s), "ac")
            p.wait("tensor", "ac", cv)
            if gu == 0:
                p.wait("tensor", "ac", gate_rel)
            else:
                p.wait("tensor", "dv", up_rel)
            tgt = psG if gu == 0 else psU
            mmv = None
            for tt in range(4):
                for kc in range(8):
                    def mm(e, o=tgt[:, tt * 512:(tt + 1) * 512],
                           w=wbuf[:, kc * 128:(kc + 1) * 128],
                           r=PB[:, kc * 2048 + tt * 512:
                                kc * 2048 + tt * 512 + 512],
                           st=(kc == 0), sp=(kc == 7)):
                        return e.matmul(o, w, r, start=st, stop=sp,
                                        skip_group_check=True)
                    last = (tt == 3) and (kc == 7)
                    mmv = p.op("tensor", mm, "pe" if last else None)
            done[gu] = mmv
            p.wait("scalar", "pe", mmv)
        p.wait("scalar", "dv", silu_rel)
        gate_rel = p.op("scalar", lambda e, i=i: e.activation(
            out=silu_t, in_=psG[:], func=AF.Silu,
            bias=b1s[:, i:i + 1]), "ac")
        p.wait("vector", "pe", done[1])
        p.wait("vector", "ac", gate_rel)
        p.wait("vector", "sA", act_st)
        up_rel = p.op("vector", lambda e, i=i: e.scalar_tensor_tensor(
            out=actsl, in0=psU[:], scalar=b1s[:, 86 + i:86 + i + 1],
            in1=silu_t, op0=ALU.add, op1=ALU.mult), "dv")
        silu_rel = up_rel
        p.wait("gpsimd", "dv", up_rel)
        for tt in range(4):
            act_st = p.dma("gpsimd", act_scr[i][tt],
                           actsl[:, tt * 512:(tt + 1) * 512], "sA")
    ph.close()


def ph_z(p, nc, V, u2_b, act_scr, z_scr):
    Abf, Bbf, Bf32 = V["Abf"], V["Bbf"], V["Bf32"]
    ph = ExitStack()
    psZ2 = ph.enter_context(_pt(nc, "psZ2", [128, 2048], F32))
    U2H = Abf[:, 0:86 * 512]
    stage = Bf32[:, 0:512]
    aslab = Bbf[:, 1024:1536]
    zslab = Bbf[:, 1536:2048]
    amv = 0
    zst = 0
    ev = 0
    for rh in range(2):
        cv = 0
        if rh == 1:
            p.wait("scalar", "pe", p.cnt["pe"])
        for kc in range(86):
            p.wait("sync", "ac", cv)
            wv = p.dma("sync", stage, u2_b[rh, kc], "wA")
            p.wait("scalar", "wA", wv)
            cv = p.op("scalar", lambda e, s=stage, kc=kc: e.copy(
                out=U2H[:, kc * 512:(kc + 1) * 512], in_=s), "ac")
        p.wait("tensor", "ac", cv)
        for tt in range(4):
            p.wait("tensor", "dv", ev)
            for kc in range(86):
                p.wait("sync", "pe", amv)
                av = p.dma("sync", aslab, act_scr[kc][tt], "qA")
                p.wait("tensor", "qA", av)
                for rc in range(4):
                    def mm(e, o=psZ2[:, rc * 512:(rc + 1) * 512],
                           w=U2H[:, kc * 512 + rc * 128:
                                 kc * 512 + rc * 128 + 128],
                           r=aslab, st=(kc == 0), sp=(kc == 85)):
                        return e.matmul(o, w, r, start=st, stop=sp,
                                        skip_group_check=True)
                    amv = p.op("tensor", mm, "pe" if rc == 3 else None)
            p.wait("vector", "pe", amv)
            for rc in range(4):
                p.wait("vector", "sA", zst)
                zv = p.op("vector", lambda e, rc=rc: e.tensor_copy(
                    out=zslab, in_=psZ2[:, rc * 512:(rc + 1) * 512]), "dv")
                p.wait("gpsimd", "dv", zv)
                zst = p.dma("gpsimd",
                            z_scr[rh * 4 + rc][:, tt * 512:(tt + 1) * 512],
                            zslab, "sA")
            ev = p.cnt["dv"]
    ph.close()


def ph_dequant(p, nc, V, codesT, codes_bf, wb_dram, sm):
    Bbf, Bf32, Bi32 = V["Bbf"], V["Bf32"], V["Bi32"]
    ph = ExitStack()
    psBC = [ph.enter_context(_pt(nc, "psBC", [128, 512], F32))
            for _ in range(2)]
    psWd = ph.enter_context(_pt(nc, "psWd", [8, 512], F32))
    psSc = ph.enter_context(_pt(nc, "psSc", [8, 512], F32))
    ci32 = Bi32[:, 0:4096]
    cf32 = Bf32[:, 4096:8192]
    cbf = Bbf[:, 16384:20480]
    strip = Bbf[0:1, 1024:9216]
    oh = Bbf[:, 20480:22528]
    wline = Bbf[0:8, 22528:26624]
    iot, e01, cbf32 = sm["iot"], sm["e01"], sm["cbf32"]
    scalef, scaleB = sm["scalef"], sm["scaleB"]
    ones_r, ones8f = sm["ones_r"], sm["ones8f"]

    v = p.op("vector", lambda e: e.tensor_copy(out=e01[:], in_=cbf32[:]), "dv")
    sv = p.op("tensor", lambda e: e.matmul(
        psSc[:], ones8f[:], scalef[:], start=True, stop=True,
        skip_group_check=True), "pe")
    p.wait("vector", "pe", sv)
    sv = p.op("vector", lambda e: e.tensor_copy(out=scaleB[:],
                                                in_=psSc[:]), "dv")
    cl = p.dma("sync", ci32,
               codesT.rearrange("j c o -> (j c o)").rearrange(
                   "(q n) -> q n", q=128), "rA")
    p.wait("vector", "rA", cl)
    v1 = p.op("vector", lambda e: e.tensor_copy(out=cf32, in_=ci32), "dv")
    p.wait("vector", "dv", v1)
    v2 = p.op("vector", lambda e: e.tensor_copy(out=cbf, in_=cf32), "dv")
    p.wait("gpsimd", "dv", v2)
    v3 = p.dma("gpsimd",
               codes_bf.rearrange("j c o -> (j c o)").rearrange(
                   "(q n) -> q n", q=128), cbf, "sA")
    p.wait("sync", "sA", v3)
    p.barrier()

    strip_pe = 0
    oh_dv = 0
    wl_st = 0
    for jg in range(64):
        p.wait("sync", "pe", strip_pe)
        stv = p.dma("sync", strip,
                    codes_bf[jg * 8:(jg + 1) * 8].rearrange(
                        "j c o -> (j c o)")[None, :], "qA")
        p.wait("tensor", "qA", stv)
        for jj in range(8):
            for cb in range(2):
                p.wait("tensor", "dv", oh_dv)
                strip_pe = p.op(
                    "tensor", lambda e, jj=jj, cb=cb: e.matmul(
                        psBC[cb][:], ones_r[:],
                        strip[0:1, (jj * 2 + cb) * 512:
                              (jj * 2 + cb) * 512 + 512],
                        start=True, stop=True, skip_group_check=True), "pe")
            p.wait("vector", "pe", strip_pe)
            for cc in range(4):
                oh_dv = p.op(
                    "vector", lambda e, cc=cc:
                    e.tensor_scalar(out=oh[:, cc * 512:(cc + 1) * 512],
                                    in0=psBC[cc // 2][:],
                                    scalar1=iot[:, (cc % 2):(cc % 2) + 1],
                                    scalar2=None, op0=ALU.is_equal), "dv")
            p.wait("tensor", "dv", oh_dv)
            wdv = None
            for cc in range(4):
                wdv = p.op("tensor", lambda e, cc=cc: e.matmul(
                    psWd[:], e01[:, cc * 8:(cc + 1) * 8],
                    oh[:, cc * 512:(cc + 1) * 512],
                    start=(cc == 0), stop=(cc == 3), skip_group_check=True),
                    "pe" if cc == 3 else None)
            p.wait("vector", "pe", wdv)
            p.wait("vector", "gC", wl_st)
            wlv = p.op("vector", lambda e, jj=jj: e.tensor_tensor(
                out=wline[:, jj * 512:(jj + 1) * 512],
                in0=psWd[:], in1=scaleB[:], op=ALU.mult), "dv")
            p.wait("tensor", "dv", wlv)
        wl_done = p.cnt["dv"]
        p.wait("gpsimd", "dv", wl_done)
        wl_st = p.dma(
            "gpsimd",
            wb_dram[jg * 64:(jg + 1) * 64, :].rearrange(
                "(g j) o -> j g o", j=8),
            wline.rearrange("j (g o) -> j g o", g=8), "gC")
    ph.close()


def _smalls(nc, es, p, names):
    sm = {}
    mk = {
        "ident": ([128, 128], BF16), "ones_c": ([128, 1], BF16),
        "ones_r": ([1, 128], BF16), "negones_r": ([1, 128], BF16),
        "ones8f": ([1, 8], F32),
        "lnrows": ([1, 1024], F32), "lnrow_h": ([1, 4096], BF16),
        "cs_h": ([1, 1], BF16), "urow_h": ([1, 512], BF16),
        "iot": ([128, 2], F32), "e01": ([128, 32], BF16),
        "cbf32": ([128, 32], F32), "scalef": ([1, 512], F32),
        "scaleB": ([8, 512], F32), "bias32": ([128, 32], F32),
        "b1s": ([128, 172], F32),
    }
    for n in names:
        sm[n] = es.enter_context(
            nc.sbuf_tensor(f"{n}_{next(_uid)}", *mk[n]))[:]
    memsets = {"ones_c": 1.0, "ones_r": 1.0, "negones_r": -1.0, "ones8f": 1.0}
    mv = None
    for n in names:
        if n in memsets:
            mv = p.op("vector", lambda e, t=sm[n], v=memsets[n]:
                      e.memset(t[:], v), "dv")
    if mv is not None:
        p.barrier()
    return sm


def _din(nc, name, shape, dt=F32):
    return nc.dram_tensor(name, shape, dt, kind="ExternalInput").ap()


def _dout(nc, name, shape, dt=BF16):
    return nc.dram_tensor(name, shape, dt, kind="ExternalOutput").ap()


def build_qkv0():
    nc, es, p, V = _mk(131072, 32768)
    x_fm = _din(nc, "x_fm", [D, S])
    wq_b = _din(nc, "wq_b", [DC, 128, D])
    wk_b = _din(nc, "wk_b", [DC, 128, D])
    wv_b = _din(nc, "wv_b", [DC, 128, D])
    xbf = _dout(nc, "xbf", [DC, 128, S])
    q_scr = _dout(nc, "q_scr", [NH, 128, S])
    k_scr = _dout(nc, "k_scr", [NH, 128, S])
    v_scr = _dout(nc, "v_scr", [NH, 128, S])
    Abf = V["Abf"]

    def xch(kc, ts):
        return Abf[:, kc * S:(kc + 1) * S][:, ts]

    ph_load_x(p, V, x_fm, xbf)
    p.barrier()
    ph_proj(p, nc, V, wq_b, DC, NH, xch, q_scr, scale=INVSQ)
    p.barrier()
    ph_proj(p, nc, V, wk_b, DC, NH, xch, k_scr)
    p.barrier()
    ph_proj(p, nc, V, wv_b, DC, NH, xch, v_scr)
    p.barrier()
    p.emit()
    return nc, es


def build_attn(layer):
    nc, es, p, V = _mk(16384, 54272)
    sm = _smalls(nc, es, p, ["ident", "ones_c", "ones_r", "negones_r",
                             "cs_h", "urow_h"])
    cD_in = _din(nc, "cD_in", [128, S], BF16)
    sD_in = _din(nc, "sD_in", [128, S], BF16)
    ident_in = _din(nc, "ident_in", [128, 128], BF16)
    attn_scr = _dout(nc, "attn_scr", [DC, 128, S])
    Abf = V["Abf"]
    p.dma("sync", Abf[:, 0:2048], cD_in, "rA")
    p.dma("sync", Abf[:, 2048:4096], sD_in, "rA")
    p.dma("sync", sm["ident"][:], ident_in, "rA")
    if layer == 0:
        q_scr = _din(nc, "q_scr", [NH, 128, S], BF16)
        k_scr = _din(nc, "k_scr", [NH, 128, S], BF16)
        v_scr = _din(nc, "v_scr", [NH, 128, S], BF16)
        heads = [(h, q_scr[h], k_scr[h], v_scr[h], h, h, False)
                 for h in range(NH)]
    else:
        masks_in = _din(nc, "masks_in", [128, S], BF16)
        p.dma("sync", Abf[:, 4096:6144], masks_in, "rA")
        qkv_scr = _din(nc, "qkv_scr", [48, 128, S], BF16)
        heads = [(4 * hk + g, qkv_scr[4 * hk + g], qkv_scr[32 + hk],
                  qkv_scr[40 + hk], 4 * hk + g, hk, True)
                 for hk in range(8) for g in range(4)]
    p.barrier()
    ph_attention(p, nc, V, layer, heads, attn_scr, sm)
    p.barrier()
    p.emit()
    return nc, es


def build_resid_ln(name_w, KC, rhs_name, rhs_nch, xold_name, out_f32,
                   bias=False, wbf16=False):
    nc, es, p, V = _mk(131072, 45056)
    names = ["ones_c", "ones_r", "lnrows", "lnrow_h"]
    if bias:
        names.append("bias32")
    sm = _smalls(nc, es, p, names)
    w_hbm = _din(nc, name_w, [DC, 128, KC * 128], BF16 if wbf16 else F32)
    rhs = _din(nc, rhs_name, [rhs_nch, 128, S], BF16)
    xold = _din(nc, xold_name, [DC, 128, S], BF16) if xold_name else None
    bias_t = _din(nc, "bias_t", [128, DC]) if bias else None
    resid_scr = nc.dram_tensor(f"resid_{next(_uid)}", [DC, 128, S],
                               BF16).ap()
    if out_f32:
        out = nc.dram_tensor("y", [D, S], F32, kind="ExternalOutput").ap()
    else:
        out = _dout(nc, "xout", [DC, 128, S])
    Abf = V["Abf"]
    ph_load_A(p, Abf, [rhs[c] for c in range(rhs_nch)])
    if bias:
        p.dma("sync", sm["bias32"][:], bias_t, "rA")
    p.barrier()

    def rhs_chunk(kc, ts):
        return Abf[:, kc * S:(kc + 1) * S][:, ts]

    ph_resid_ln(p, nc, V, w_hbm, KC, rhs_chunk, xold, out, out_f32, sm,
                bias_cols=sm["bias32"] if bias else None, wbf16=wbf16,
                resid_scr=resid_scr)
    p.barrier()
    p.emit()
    return nc, es


def build_p0():
    nc, es, p, V = _mk(131072, 32768)
    x1 = _din(nc, "x1", [DC, 128, S], BF16)
    u1_b = _din(nc, "u1_b", [8, 128, D])
    p_scr = _dout(nc, "p_scr", [8, 128, S])
    Abf = V["Abf"]
    ph_load_A(p, Abf, [x1[c] for c in range(DC)])
    p.barrier()

    def xch(kc, ts):
        return Abf[:, kc * S:(kc + 1) * S][:, ts]

    ph_proj(p, nc, V, u1_b, DC, 8, xch, p_scr)
    p.barrier()
    p.emit()
    return nc, es


def build_swiglu():
    nc, es, p, V = _mk(32768, 16384)
    sm = _smalls(nc, es, p, ["b1s"])
    p_scr = _din(nc, "p_scr", [8, 128, S], BF16)
    v1_b = _din(nc, "v1_b", [172, 128, 1024])
    b1T = _din(nc, "b1T", [128, 172])
    act_scr = _dout(nc, "act_scr", [86, 4, 128, 512])
    p.dma("sync", sm["b1s"][:], b1T, "rA")
    p.barrier()
    ph_swiglu(p, nc, V, v1_b, sm["b1s"], [p_scr[i] for i in range(8)],
              act_scr)
    p.barrier()
    p.emit()
    return nc, es


def build_z():
    nc, es, p, V = _mk(90112, 8192)
    act_scr = _din(nc, "act_scr", [86, 4, 128, 512], BF16)
    u2_b = _din(nc, "u2_b", [2, 86, 128, 512])
    z_scr = _dout(nc, "z_scr", [8, 128, S])
    ph_z(p, nc, V, u2_b, act_scr, z_scr)
    p.barrier()
    p.emit()
    return nc, es


def build_qkv1():
    nc, es, p, V = _mk(131072, 32768)
    x2 = _din(nc, "x2", [DC, 128, S], BF16)
    win_b = _din(nc, "win_b", [48, 128, D])
    qkv_scr = _dout(nc, "qkv_scr", [48, 128, S])
    Abf = V["Abf"]
    ph_load_A(p, Abf, [x2[c] for c in range(DC)])
    p.barrier()

    def xch(kc, ts):
        return Abf[:, kc * S:(kc + 1) * S][:, ts]

    ph_proj(p, nc, V, win_b, DC, 48, xch, qkv_scr)
    p.barrier()
    p.emit()
    return nc, es


def build_dequant():
    nc, es, p, V = _mk(0, 61440)
    sm = _smalls(nc, es, p, ["iot", "e01", "cbf32", "scalef", "scaleB",
                             "ones_r", "ones8f"])
    codesT = _din(nc, "codesT", [512, 2, 512], I32)
    cbE = _din(nc, "cbE", [128, 32])
    myscale = _din(nc, "myscale", [1, 512])
    iot_in = _din(nc, "iot_in", [128, 2])
    wb_dram = _dout(nc, "wb_dram", [4096, 512])
    codes_bf = nc.dram_tensor("codes_bf", [512, 2, 512], BF16).ap()
    p.dma("sync", sm["cbf32"][:], cbE, "rA")
    p.dma("sync", sm["scalef"][:], myscale, "rA")
    p.dma("sync", sm["iot"][:], iot_in, "rA")
    p.barrier()
    ph_dequant(p, nc, V, codesT, codes_bf, wb_dram, sm)
    p.barrier()
    p.emit()
    return nc, es


def build_fused(mock_cc=False):
    nc, es, p, V = _mk(131072, 57344)
    Abf, Bbf, Bf32 = V["Abf"], V["Bbf"], V["Bf32"]
    sm = _smalls(nc, es, p, ["ident", "ones_c", "ones_r", "negones_r",
                             "ones8f", "cs_h", "urow_h", "iot", "e01",
                             "cbf32", "bias32", "b1s"])
    smR = dict(sm)
    smR["lnrows"] = Bf32[0:1, 10752:11776]
    smR["lnrow_h"] = Bbf[0:1, 23552:27648]
    smD = dict(sm)
    smD["scalef"] = Bf32[0:1, 13312:13824]
    smD["scaleB"] = Bf32[0:8, 13824:14336]

    x_fm = _din(nc, "x_fm", [D, S])
    wq_b = _din(nc, "wq_b", [DC, 128, D])
    wk_b = _din(nc, "wk_b", [DC, 128, D])
    wv_b = _din(nc, "wv_b", [DC, 128, D])
    wo_b = _din(nc, "wo_b", [DC, 128, D])
    u1_b = _din(nc, "u1_b", [8, 128, D])
    v1_b = _din(nc, "v1_b", [172, 128, 1024])
    u2_b = _din(nc, "u2_b", [2, 86, 128, 512])
    v2_b = _din(nc, "v2_b", [DC, 128, 1024])
    win_b = _din(nc, "win_b", [48, 128, D])
    wout_b = _din(nc, "wout_b", [DC, 128, D])
    b1T = _din(nc, "b1T", [128, 172])
    b2T = _din(nc, "b2T", [128, DC])
    bL1T = _din(nc, "bL1T", [128, DC])
    codesT = _din(nc, "codesT", [512, 2, 512], I32)
    cbE = _din(nc, "cbE", [128, 32])
    myscale = _din(nc, "myscale", [1, 512])
    iot_in = _din(nc, "iot_in", [128, 2])
    cD_in = _din(nc, "cD_in", [128, S], BF16)
    sD_in = _din(nc, "sD_in", [128, S], BF16)
    ident_in = _din(nc, "ident_in", [128, 128], BF16)
    masks_in = _din(nc, "masks_in", [128, S], BF16)
    y = nc.dram_tensor("y", [D, S], F32, kind="ExternalOutput").ap()

    def scr(name, shape, dt=BF16):
        return nc.dram_tensor(name, shape, dt).ap()

    xbf = scr("xbf", [DC, 128, S])
    q_scr = scr("q_scr", [NH, 128, S])
    k_scr = scr("k_scr", [NH, 128, S])
    v_scr = scr("v_scr", [NH, 128, S])
    attn_scr = scr("attn_scr", [DC, 128, S])
    x1 = scr("x1", [DC, 128, S])
    p_scr = scr("p_scr", [8, 128, S])
    act_scr = scr("act_scr", [86, 4, 128, 512])
    z_scr = scr("z_scr", [8, 128, S])
    x2 = scr("x2", [DC, 128, S])
    qkv_scr = scr("qkv_scr", [48, 128, S])
    x3 = scr("x3", [DC, 128, S])
    resid_scr = scr("resid_scr", [DC, 128, S])
    codes_bf = scr("codes_bf", [512, 2, 512])
    wb_dram = scr("wb_dram", [4096, 512])
    wfull = scr("wfull", [8 * 4096, 512])
    wfin4 = scr("wfin4", [DC, 128, 32, 128])

    def xch(kc, ts):
        return Abf[:, kc * S:(kc + 1) * S][:, ts]

    p.dma("sync", sm["ident"][:], ident_in, "rA")
    p.dma("sync", sm["iot"][:], iot_in, "rA")
    p.dma("sync", sm["cbf32"][:], cbE, "rA")
    p.dma("sync", sm["b1s"][:], b1T, "rA")
    p.dma("sync", smD["scalef"][:], myscale, "rA")
    p.barrier()

    ph_dequant(p, nc, V, codesT, codes_bf, wb_dram, smD)
    p.barrier()
    if mock_cc:
        for r_ in range(8):
            p.dma("gpsimd", wfull[r_ * 4096:(r_ + 1) * 4096, :], wb_dram,
                  "cc")
        p.wait("gpsimd", "cc", p.cnt["cc"])
    else:
        p.op("gpsimd", lambda e: e.collective_compute(
            "AllGather", ALU.bypass, replica_groups=[list(range(8))],
            ins=[wb_dram], outs=[wfull]), "cc")
        p.wait("gpsimd", "cc", 1)
    p.wait("scalar", "cc", p.cnt["cc"] if mock_cc else 1)
    for oc in range(DC):
        r = oc // 4
        csl = slice((oc % 4) * 128, (oc % 4) * 128 + 128)
        for qtr in range(4):
            src = wfull[r * 4096 + qtr * 1024:r * 4096 + (qtr + 1) * 1024,
                        csl].rearrange("(a q) m -> q a m", q=128)
            p.dma("scalar", wfin4[oc][:, qtr * 8:(qtr + 1) * 8, :], src, "rB")
    p.barrier()

    ph_load_x(p, V, x_fm, xbf)
    p.barrier()
    ph_proj(p, nc, V, wq_b, DC, NH, xch, q_scr, scale=INVSQ)
    p.barrier()
    ph_proj(p, nc, V, wk_b, DC, NH, xch, k_scr)
    p.barrier()
    ph_proj(p, nc, V, wv_b, DC, NH, xch, v_scr)
    p.barrier()

    p.dma("sync", Abf[:, 0:2048], cD_in, "rA")
    p.dma("sync", Abf[:, 2048:4096], sD_in, "rA")
    p.barrier()
    heads0 = [(h, q_scr[h], k_scr[h], v_scr[h], h, h, False)
              for h in range(NH)]
    ph_attention(p, nc, V, 0, heads0, attn_scr, sm)
    p.barrier()

    ph_load_A(p, Abf, [attn_scr[c] for c in range(DC)])
    p.barrier()
    ph_resid_ln(p, nc, V, wo_b, DC, xch, xbf, x1, False, smR,
                resid_scr=resid_scr)
    p.barrier()

    ph_load_A(p, Abf, [x1[c] for c in range(DC)])
    p.barrier()
    ph_proj(p, nc, V, u1_b, DC, 8, xch, p_scr)
    p.barrier()
    ph_swiglu(p, nc, V, v1_b, sm["b1s"], [p_scr[i] for i in range(8)],
              act_scr)
    p.barrier()
    ph_z(p, nc, V, u2_b, act_scr, z_scr)
    p.barrier()
    ph_load_A(p, Abf, [z_scr[c] for c in range(8)])
    p.dma("sync", sm["bias32"][:], b2T, "rA")
    p.barrier()
    ph_resid_ln(p, nc, V, v2_b, 8, xch, x1, x2, False, smR,
                bias_cols=sm["bias32"], resid_scr=resid_scr)
    p.barrier()

    ph_load_A(p, Abf, [x2[c] for c in range(DC)])
    p.barrier()
    ph_proj(p, nc, V, win_b, DC, 48, xch, qkv_scr)
    p.barrier()

    p.dma("sync", Abf[:, 0:2048], cD_in, "rA")
    p.dma("sync", Abf[:, 2048:4096], sD_in, "rA")
    p.dma("sync", Abf[:, 4096:6144], masks_in, "rA")
    p.barrier()
    heads1 = [(4 * hk + g, qkv_scr[4 * hk + g], qkv_scr[32 + hk],
               qkv_scr[40 + hk], 4 * hk + g, hk, True)
              for hk in range(8) for g in range(4)]
    ph_attention(p, nc, V, 1, heads1, attn_scr, sm)
    p.barrier()

    ph_load_A(p, Abf, [attn_scr[c] for c in range(DC)])
    p.barrier()
    ph_resid_ln(p, nc, V, wout_b, DC, xch, x2, x3, False, smR,
                resid_scr=resid_scr)
    p.barrier()

    ph_load_A(p, Abf, [x3[c] for c in range(DC)])
    p.dma("sync", sm["bias32"][:], bL1T, "rA")
    p.barrier()
    wfin_v = [wfin4[oc].rearrange("q a m -> q (a m)") for oc in range(DC)]
    ph_resid_ln(p, nc, V, wfin_v, DC, xch, None, y, True, smR,
                bias_cols=sm["bias32"], wbf16=True, resid_scr=resid_scr)
    p.barrier()
    p.emit()
    return nc, es


FUSED = True


def kernel_fused(inputs):
    sh, percore = _prep(**inputs)
    keys = ["wq_b", "wk_b", "wv_b", "wo_b", "u1_b", "v1_b", "u2_b", "v2_b",
            "win_b", "wout_b", "b1T", "b2T", "bL1T", "cbE", "iot_in",
            "cD_in", "sD_in", "ident_in", "masks_in"]
    in_maps = _maps(sh, percore, keys, ["x_fm", "codesT", "myscale"])
    res = _run(build_fused, in_maps)
    out = np.empty((8, S, D), np.float32)
    for r in range(8):
        out[r] = res[r]["y"].T
    return out


def _blk(W_io, OC):
    KIN = W_io.shape[0]
    KC = KIN // 128
    M = W_io.shape[1] // OC
    r = W_io.reshape(KC, 128, OC, M).transpose(2, 1, 0, 3)
    return np.ascontiguousarray(r.reshape(OC, 128, KC * M))


def _consts():
    inv = 1.0 / (10000.0 ** (np.arange(0, HD, 2) / HD))
    t = np.arange(S, dtype=np.float64)
    ang = np.outer(t, inv)
    c = np.cos(ang).astype(np.float32).T
    s = np.sin(ang).astype(np.float32).T
    cd = np.concatenate([c, c], 0).astype(BF)
    sd = np.concatenate([-s, s], 0).astype(BF)
    ident = np.eye(128, dtype=BF)
    masksv = np.zeros((128, 2048), np.float32)
    for mi, delta in enumerate([0, 128, 256, 384]):
        pp = np.arange(128)[:, None]
        ff = np.arange(512)[None, :]
        masksv[:, mi * 512:(mi + 1) * 512] = (pp + delta <= ff)
    iotv = np.stack([np.arange(128, dtype=np.float32),
                     np.arange(128, 256, dtype=np.float32)], 1)
    return cd, sd, ident, np.ascontiguousarray(masksv.astype(BF)), iotv


def _prep(L0_attn_x, L0_attn_Wq, L0_attn_Wk, L0_attn_Wv, L0_attn_Wo,
          L0_ffn_U1, L0_ffn_V1, L0_ffn_U2, L0_ffn_V2, L0_ffn_b1, L0_ffn_b2,
          L1_attn_w_in, L1_attn_w_out, L1_ffn_codes, L1_ffn_codebooks,
          L1_ffn_scales, L1_ffn_bias):
    f = np.asarray
    wq = f(L0_attn_Wq, np.float32)
    wk = f(L0_attn_Wk, np.float32)
    wv = f(L0_attn_Wv, np.float32)
    wo = f(L0_attn_Wo, np.float32)
    u1 = f(L0_ffn_U1, np.float32)
    v1 = f(L0_ffn_V1, np.float32)
    u2 = f(L0_ffn_U2, np.float32)
    v2 = f(L0_ffn_V2, np.float32)
    b1 = f(L0_ffn_b1, np.float32)
    b2 = f(L0_ffn_b2, np.float32)
    win = f(L1_attn_w_in, np.float32)
    wout = f(L1_attn_w_out, np.float32)
    codes = f(L1_ffn_codes, np.int32)
    cb = f(L1_ffn_codebooks, np.float32)
    scales = f(L1_ffn_scales, np.float32)
    bias = f(L1_ffn_bias, np.float32)
    x = f(L0_attn_x, np.float32)

    cd, sd, identv, masksv, iotv = _consts()
    sh = dict(
        wq_b=_blk(wq.T, DC), wk_b=_blk(wk.T, DC), wv_b=_blk(wv.T, DC),
        wo_b=_blk(wo.T, DC), u1_b=_blk(u1, 8), v1_b=_blk(v1, 172),
        u2_b=np.ascontiguousarray(
            u2.reshape(86, 128, 2, 512).transpose(2, 0, 1, 3)),
        v2_b=_blk(v2, DC), win_b=_blk(win.T, 48), wout_b=_blk(wout.T, DC),
        b1T=np.ascontiguousarray(b1.reshape(172, 128).T),
        b2T=np.ascontiguousarray(b2.reshape(32, 128).T),
        bL1T=np.ascontiguousarray(bias.reshape(32, 128).T),
        cbE=np.ascontiguousarray(cb.reshape(512, 8).reshape(4, 128, 8)
                                 .transpose(1, 0, 2).reshape(128, 32)),
        cD_in=cd, sD_in=sd, ident_in=identv, masks_in=masksv, iot_in=iotv,
    )
    percore = []
    for r in range(8):
        m = {}
        m["x_fm"] = np.ascontiguousarray(x[r].T)
        mc = codes[512 * r:512 * (r + 1)]
        m["codesT"] = np.ascontiguousarray(mc.transpose(1, 2, 0))
        m["myscale"] = np.ascontiguousarray(
            scales.reshape(-1)[512 * r:512 * (r + 1)].reshape(1, 512))
        percore.append(m)
    return sh, percore


def _run(builder, in_maps):
    nc, es = builder()
    res = run_bass_kernel_spmd(nc, in_maps, core_ids=list(range(8)))
    es.close()
    return res.results


def _maps(sh, percore, keys, pc_keys=()):
    out = []
    for r in range(8):
        m = {k: sh[k] for k in keys}
        for k in pc_keys:
            m[k] = percore[r][k]
        out.append(m)
    return out


def kernel(**inputs):
    if FUSED:
        return kernel_fused(inputs)
    sh, percore = _prep(**inputs)

    r1 = _run(build_qkv0,
              _maps(sh, percore, ["wq_b", "wk_b", "wv_b"], ["x_fm"]))
    r2 = _run(lambda: build_attn(0),
              [dict(cD_in=sh["cD_in"], sD_in=sh["sD_in"],
                    ident_in=sh["ident_in"], q_scr=r1[r]["q_scr"],
                    k_scr=r1[r]["k_scr"], v_scr=r1[r]["v_scr"])
               for r in range(8)])
    r3 = _run(lambda: build_resid_ln("wo_b", DC, "attn_scr", DC, "xbf",
                                     False),
              [dict(wo_b=sh["wo_b"], attn_scr=r2[r]["attn_scr"],
                    xbf=r1[r]["xbf"]) for r in range(8)])
    x1 = [r3[r]["xout"] for r in range(8)]
    r4 = _run(build_p0,
              [dict(x1=x1[r], u1_b=sh["u1_b"]) for r in range(8)])
    r5 = _run(build_swiglu,
              [dict(p_scr=r4[r]["p_scr"], v1_b=sh["v1_b"], b1T=sh["b1T"])
               for r in range(8)])
    r6 = _run(build_z,
              [dict(act_scr=r5[r]["act_scr"], u2_b=sh["u2_b"])
               for r in range(8)])
    r7 = _run(lambda: build_resid_ln("v2_b", 8, "z_scr", 8, "x1b",
                                     False, bias=True),
              [dict(v2_b=sh["v2_b"], z_scr=r6[r]["z_scr"], x1b=x1[r],
                    bias_t=sh["b2T"]) for r in range(8)])
    x2 = [r7[r]["xout"] for r in range(8)]
    r8 = _run(build_qkv1,
              [dict(x2=x2[r], win_b=sh["win_b"]) for r in range(8)])
    r9 = _run(lambda: build_attn(1),
              [dict(cD_in=sh["cD_in"], sD_in=sh["sD_in"],
                    ident_in=sh["ident_in"], masks_in=sh["masks_in"],
                    qkv_scr=r8[r]["qkv_scr"]) for r in range(8)])
    r10 = _run(lambda: build_resid_ln("wout_b", DC, "attn_scr", DC, "x2b",
                                      False),
               [dict(wout_b=sh["wout_b"], attn_scr=r9[r]["attn_scr"],
                     x2b=x2[r]) for r in range(8)])
    x3 = [r10[r]["xout"] for r in range(8)]
    r11 = _run(build_dequant,
               _maps(sh, percore, ["cbE", "iot_in"],
                     ["codesT", "myscale"]))
    WT = np.concatenate([r11[r]["wb_dram"] for r in range(8)], axis=1)
    wfin = _blk(WT, DC)
    r12 = _run(lambda: build_resid_ln("wfin", DC, "x3", DC, None,
                                      True, bias=True, wbf16=True),
               [dict(wfin=wfin, x3=x3[r], bias_t=sh["bL1T"])
                for r in range(8)])
    out = np.empty((8, S, D), np.float32)
    for r in range(8):
        out[r] = r12[r]["y"].T
    return out


# version 04 (final)
# speedup vs baseline: 1.2606x; 1.2606x over previous
import math
import itertools
import numpy as np
import ml_dtypes
from contextlib import ExitStack

import concourse.bass as bass
import concourse.mybir as mybir
from concourse.bass_utils import run_bass_kernel_spmd

F32 = mybir.dt.float32
BF16 = mybir.dt.bfloat16
I32 = mybir.dt.int32
AF = mybir.ActivationFunctionType
ALU = mybir.AluOpType
AX = mybir.AxisListType
BF = ml_dtypes.bfloat16

S = 2048
D = 4096
DC = 32
HD = 128
NH = 32
EPS = 1e-5
INVSQ = 1.0 / math.sqrt(128.0)

ENGS = ("sync", "vector", "tensor", "scalar", "gpsimd")
SEMS = ("wA", "wB", "rA", "rB", "sA", "sB", "qA", "qB",
        "gA", "gB", "gC", "gD", "pe", "dv", "ac", "gp", "cc")

_uid = itertools.count()


class P:
    def __init__(self, nc, es):
        self.nc = nc
        self.ops = {e: [] for e in ENGS}
        self.sems = {}
        self.cnt = {}
        for s in SEMS:
            self.sems[s] = es.enter_context(nc.semaphore(s))
            self.cnt[s] = 0

    def wait(self, eng, sem, val):
        if val is None or val <= 0:
            return
        s = self.sems[sem]
        self.ops[eng].append(lambda e, s=s, v=val: e.wait_ge(s, v))

    def dma(self, eng, out, in_, sem):
        s = self.sems[sem]
        self.cnt[sem] += 16
        self.ops[eng].append(
            lambda e, o=out, i=in_, s=s: e.dma_start(out=o, in_=i).then_inc(s, 16))
        return self.cnt[sem]

    def op(self, eng, fn, sem=None):
        if sem is None:
            self.ops[eng].append(fn)
            return None
        s = self.sems[sem]
        self.cnt[sem] += 1
        self.ops[eng].append(lambda e, f=fn, s=s: f(e).then_inc(s, 1))
        return self.cnt[sem]

    def barrier(self):
        snap = dict(self.cnt)
        for eng in ENGS:
            for sem, v in snap.items():
                if v > 0:
                    self.wait(eng, sem, v)

    def emit(self):
        with self.nc.Block() as block:
            for eng in ENGS:
                def run(e, L=self.ops[eng]):
                    for f in L:
                        f(e)
                getattr(block, eng)(run)


def _pt(nc, name, shape, dt=F32):
    return nc.psum_tensor(f"{name}_{next(_uid)}", shape, dt)


def _mk(a_bytes, b_bytes):
    nc = bass.Bass("TRN2", target_bir_lowering=False)
    es = ExitStack()
    p = P(nc, es)
    V = {}
    if a_bytes:
        A = es.enter_context(nc.sbuf_tensor(f"AA_{next(_uid)}",
                                            [128, a_bytes], mybir.dt.uint8))
        V["Abf"] = A[:].bitcast(BF16)
        V["Af32"] = A[:].bitcast(F32)
    if b_bytes:
        Bt = es.enter_context(nc.sbuf_tensor(f"BB_{next(_uid)}",
                                             [128, b_bytes], mybir.dt.uint8))
        V["Bbf"] = Bt[:].bitcast(BF16)
        V["Bf32"] = Bt[:].bitcast(F32)
        V["Bi32"] = Bt[:].bitcast(I32)
    return nc, es, p, V


def ph_load_A(p, Abf, srcs):
    for c, src in enumerate(srcs):
        p.dma("sync", Abf[:, c * S:(c + 1) * S], src, "rA")


def ph_load_x(p, V, x_fm, xbf):
    Abf, Bf32 = V["Abf"], V["Bf32"]
    stage = Bf32[:, 0:2048]
    cv = 0
    for c in range(DC):
        p.wait("sync", "ac", cv)
        wv = p.dma("sync", stage, x_fm[c * 128:(c + 1) * 128, :], "wA")
        p.wait("scalar", "wA", wv)
        dst = Abf[:, c * S:(c + 1) * S]
        cv = p.op("scalar", lambda e, s=stage, d=dst: e.copy(out=d, in_=s), "ac")
        p.wait("gpsimd", "ac", cv)
        p.dma("gpsimd", xbf[c], dst, "sA")


def ph_proj(p, nc, V, w_hbm, KC, OC, rhs_chunk, dest, scale=None):
    Bbf, Bf32 = V["Bbf"], V["Bf32"]
    ph = ExitStack()
    ps = [ph.enter_context(_pt(nc, "pj", [128, 2048], F32)) for _ in range(2)]
    stage = [Bf32[:, 0:KC * 128], Bf32[:, 4096:4096 + KC * 128]]
    wbuf = [Bbf[:, 16384:16384 + KC * 128], Bbf[:, 20480:20480 + KC * 128]]
    slabs = [Bbf[:, 24576:26624], Bbf[:, 26624:28672]]
    wsem = ["wA", "wB"]
    ssem = ["sA", "sB"]
    stage_free = [0, 0]
    wbuf_free = [0, 0]
    ps_free = [0, 0]
    slab_free = [0, 0]
    for oc in range(OC):
        sl = oc % 2
        p.wait("sync", "ac", stage_free[sl])
        wv = p.dma("sync", stage[sl], w_hbm[oc], wsem[sl])
        p.wait("scalar", wsem[sl], wv)
        p.wait("scalar", "pe", wbuf_free[sl])
        stage_free[sl] = p.op(
            "scalar", lambda e, s=stage[sl], w=wbuf[sl]: e.copy(out=w, in_=s),
            "ac")
        p.wait("tensor", "ac", stage_free[sl])
        p.wait("tensor", "ac", ps_free[sl])
        mmv = None
        for tt in range(4):
            for kc in range(KC):
                def mm(e, o=ps[sl][:, tt * 512:(tt + 1) * 512],
                       w=wbuf[sl][:, kc * 128:(kc + 1) * 128],
                       r=rhs_chunk(kc, slice(tt * 512, tt * 512 + 512)),
                       st=(kc == 0), sp=(kc == KC - 1)):
                    return e.matmul(o, w, r, start=st, stop=sp,
                                    skip_group_check=True)
                last = (tt == 3) and (kc == KC - 1)
                mmv = p.op("tensor", mm, "pe" if last else None)
        wbuf_free[sl] = mmv
        p.wait("scalar", "pe", mmv)
        p.wait("scalar", ssem[sl], slab_free[sl])
        if scale is None:
            ps_free[sl] = p.op(
                "scalar", lambda e, s=ps[sl][:], d=slabs[sl]: e.copy(
                    out=d, in_=s), "ac")
        else:
            ps_free[sl] = p.op(
                "scalar", lambda e, s=ps[sl][:], d=slabs[sl]: e.mul(
                    out=d, in_=s, mul=scale), "ac")
        p.wait("gpsimd", "ac", ps_free[sl])
        slab_free[sl] = p.dma("gpsimd", dest[oc], slabs[sl], ssem[sl])
    ph.close()


def ph_attention(p, nc, V, layer, heads, dest, sm):
    Abf, Bbf, Bf32 = V["Abf"], V["Bbf"], V["Bf32"]
    ph = ExitStack()
    psS = ph.enter_context(_pt(nc, "psS", [128, 2048], F32))
    psPV = ph.enter_context(_pt(nc, "psPV", [128, 512], F32))
    psZ = ph.enter_context(_pt(nc, "psZ", [1, 512], F32))
    psT = ph.enter_context(_pt(nc, "psT", [128, 256], BF16))
    psR = ph.enter_context(_pt(nc, "psR", [1, 512], F32))

    o = {}
    off = 0
    for nm, n in [("P1q", S), ("P2q", S), ("P1k", S), ("P2k", S),
                  ("Vt", S), ("QROT", S), ("KROT", S), ("SQq", S),
                  ("SQk", S), ("vT", S), ("pT", S),
                  ("outA", 512), ("outB", 512)]:
        o[nm] = Bbf[:, off:off + n]
        off += n
    cD = Abf[:, 0:2048]
    sD = Abf[:, 2048:4096]
    masks = Abf[:, 4096:6144]
    Ccol = Bf32[:, 12100:12101]
    sc1 = Bf32[0:1, 12102:12103]
    sc2 = Bf32[0:1, 12104:12105]
    kt = Bf32[0:1, 12106:12107]
    urow_f = Bf32[0:1, 12544:13056]
    uBf = Bf32[:, 13056:13568]
    cs_h = sm["cs_h"]
    urow_h = sm["urow_h"]
    ones_c, ones_r, negones_r = sm["ones_c"], sm["ones_r"], sm["negones_r"]
    ident = sm["ident"]

    out_free = [0, 0]
    n_out = 0
    prev_kv = -1
    for (h, qsrc, ksrc, vsrc, ochunk, kvid, causal) in heads:
        new_kv = (kvid != prev_kv)
        prev_kv = kvid
        p.barrier()
        p.dma("sync", o["P1q"], qsrc, "rA")
        p.dma("sync", o["P2q"][0:64, :], qsrc[64:128, :], "rA")
        lv = p.dma("sync", o["P2q"][64:128, :], qsrc[0:64, :], "rA")
        if new_kv:
            p.dma("sync", o["P1k"], ksrc, "rA")
            p.dma("sync", o["P2k"][0:64, :], ksrc[64:128, :], "rA")
            p.dma("sync", o["P2k"][64:128, :], ksrc[0:64, :], "rA")
            lv = p.dma("sync", o["Vt"], vsrc, "rA")
        p.wait("vector", "rA", lv)

        def rot(dst, p1, p2, tmp):
            p.op("vector", lambda e, dst=dst, p1=p1: e.tensor_mul(
                out=dst, in0=p1, in1=cD), "dv")
            mv = p.op("vector", lambda e, tmp=tmp, p2=p2: e.tensor_mul(
                out=tmp, in0=p2, in1=sD), "dv")
            p.wait("vector", "dv", mv)
            return p.op("vector", lambda e, dst=dst, tmp=tmp: e.tensor_add(
                out=dst, in0=dst, in1=tmp), "dv")

        rq = rot(o["QROT"], o["P1q"], o["P2q"], o["SQq"])
        if layer == 1:
            p.wait("vector", "dv", rq)
            rq = p.op("vector", lambda e: e.tensor_scalar(
                out=o["QROT"], in0=o["QROT"], scalar1=INVSQ, scalar2=None,
                op0=ALU.mult), "dv")
        if new_kv:
            rot(o["KROT"], o["P1k"], o["P2k"], o["SQk"])
        rot_done = p.cnt["dv"]

        p.wait("scalar", "dv", rot_done)
        p.op("scalar", lambda e: e.activation(out=o["SQq"], in_=o["QROT"],
                                              func=AF.Square), "ac")
        if new_kv:
            p.op("scalar", lambda e: e.activation(out=o["SQk"], in_=o["KROT"],
                                                  func=AF.Square), "ac")
        sq_done = p.cnt["ac"]
        p.wait("tensor", "ac", sq_done)

        def colmax(src, dst):
            for tt in range(4):
                mv = p.op("tensor", lambda e, tt=tt: e.matmul(
                    psR[:], ones_c[:], src[:, tt * 512:(tt + 1) * 512],
                    start=True, stop=True, skip_group_check=True), "pe")
                p.wait("vector", "pe", mv)
                if tt == 0:
                    rv = p.op("vector", lambda e: e.reduce_max(
                        out=dst, in_=psR[:], axis=AX.X), "dv")
                else:
                    rv = p.op("vector", lambda e: e.reduce_max(
                        out=kt[:], in_=psR[:], axis=AX.X), "dv")
                    p.wait("vector", "dv", rv)
                    rv = p.op("vector", lambda e: e.tensor_max(
                        out=dst, in0=dst, in1=kt[:]), "dv")
                p.wait("tensor", "dv", rv)
            return rv

        colmax(o["SQq"], sc2)
        if new_kv:
            colmax(o["SQk"], sc1)
        mx_done = p.cnt["dv"]
        p.wait("vector", "dv", mx_done)
        cv = p.op("vector", lambda e: e.tensor_tensor(
            out=sc2, in0=sc2, in1=sc1, op=ALU.mult), "dv")
        p.wait("scalar", "dv", cv)
        cv = p.op("scalar", lambda e: e.activation(
            out=cs_h[:], in_=sc2, func=AF.Sqrt), "ac")
        p.wait("tensor", "ac", cv)
        cv = p.op("tensor", lambda e: e.matmul(
            psPV[:, 0:1], negones_r[:], cs_h[:], start=True, stop=True,
            skip_group_check=True), "pe")
        p.wait("vector", "pe", cv)
        cv = p.op("vector", lambda e: e.tensor_copy(
            out=Ccol, in_=psPV[:, 0:1]), "dv")
        p.wait("scalar", "dv", cv)
        ccol_done = cv

        if new_kv:
            tv = 0
            for tc in range(16):
                tsl = tc % 2
                p.wait("tensor", "ac", tv)
                mv = p.op("tensor", lambda e, tc=tc, tsl=tsl: e.transpose(
                    psT[:, tsl * 128:(tsl + 1) * 128],
                    o["Vt"][:, tc * 128:(tc + 1) * 128], ident[:]), "pe")
                p.wait("scalar", "pe", mv)
                tv = p.op("scalar", lambda e, tc=tc, tsl=tsl: e.copy(
                    out=o["vT"][:, tc * 128:(tc + 1) * 128],
                    in_=psT[:, tsl * 128:(tsl + 1) * 128]), "ac")
            p.wait("tensor", "ac", tv)

        psS_free = {}
        pT_free = {}
        pv_rel = 0
        for st_ in range(4):
            ssl = slice(st_ * 512, (st_ + 1) * 512)
            tcs = list(range(4 * (st_ + 1))) if causal else list(range(16))
            for i, tc in enumerate(tcs):
                sb = tc % 4
                ssb = slice(sb * 512, (sb + 1) * 512)
                p.wait("tensor", "ac", psS_free.get(sb, 0))
                smm = p.op("tensor", lambda e, tc=tc, ssb=ssb, ssl=ssl:
                           e.matmul(psS[:, ssb],
                                    o["KROT"][:, tc * 128:(tc + 1) * 128],
                                    o["QROT"][:, ssl], start=True, stop=True,
                                    skip_group_check=True), "pe")
                p.wait("scalar", "pe", smm)
                p.wait("scalar", "pe", pT_free.get(sb, 0))
                ev = p.op("scalar", lambda e, ssb=ssb: e.activation(
                    out=o["pT"][:, ssb], in_=psS[:, ssb], func=AF.Exp,
                    bias=Ccol, scale=1.0), "ac")
                psS_free[sb] = ev
                dep = ("ac", ev)
                if causal:
                    delta = tc * 128 - st_ * 512
                    if 0 <= delta <= 384:
                        mi = delta // 128
                        p.wait("vector", "ac", ev)
                        mv = p.op("vector", lambda e, ssb=ssb, mi=mi:
                                  e.tensor_tensor(
                                      out=o["pT"][:, ssb],
                                      in0=o["pT"][:, ssb],
                                      in1=masks[:, mi * 512:(mi + 1) * 512],
                                      op=ALU.mult), "dv")
                        dep = ("dv", mv)
                p.wait("tensor", dep[0], dep[1])
                if i == 0:
                    p.wait("tensor", "dv", pv_rel)
                first = (i == 0)
                lastc = (i == len(tcs) - 1)
                p.op("tensor", lambda e, ssb=ssb, first=first, lastc=lastc:
                     e.matmul(psZ[:], ones_c[:], o["pT"][:, ssb],
                              start=first, stop=lastc,
                              skip_group_check=True), None)
                pT_free[sb] = p.op(
                    "tensor", lambda e, ssb=ssb, tc=tc, first=first,
                    lastc=lastc: e.matmul(
                        psPV[:], o["vT"][:, tc * 128:(tc + 1) * 128],
                        o["pT"][:, ssb], start=first, stop=lastc,
                        skip_group_check=True), "pe")
            pv_done = pT_free[tcs[-1] % 4]
            p.wait("vector", "pe", pv_done)
            v5 = p.op("vector", lambda e: e.reciprocal(out=urow_f,
                                                       in_=psZ[:]), "dv")
            p.wait("vector", "dv", v5)
            v6 = p.op("vector", lambda e: e.tensor_copy(out=urow_h[:],
                                                        in_=urow_f), "dv")
            p.wait("tensor", "dv", v6)
            p.wait("tensor", "ac", psS_free.get(0, 0))
            ub = p.op("tensor", lambda e: e.matmul(
                psS[:, 0:512], ones_r[:], urow_h[:], start=True, stop=True,
                skip_group_check=True), "pe")
            p.wait("scalar", "pe", ub)
            p.wait("scalar", "dv", pv_rel)
            ub = p.op("scalar", lambda e: e.copy(out=uBf,
                                                 in_=psS[:, 0:512]), "ac")
            psS_free[0] = ub
            osl = n_out % 2
            n_out += 1
            osem = "sA" if osl == 0 else "sB"
            obuf = o["outA"] if osl == 0 else o["outB"]
            p.wait("vector", "ac", ub)
            p.wait("vector", osem, out_free[osl])
            v8 = p.op("vector", lambda e, obuf=obuf: e.tensor_tensor(
                out=obuf, in0=psPV[:], in1=uBf, op=ALU.mult), "dv")
            pv_rel = v8
            p.wait("gpsimd", "dv", v8)
            out_free[osl] = p.dma(
                "gpsimd", dest[ochunk][:, ssl], obuf, osem)
    ph.close()


def ph_resid_ln(p, nc, V, w_hbm, KC, rhs_chunk, xold, out, out_f32, sm,
                bias_cols=None, wbf16=False, resid_scr=None):
    Abf, Bbf, Bf32 = V["Abf"], V["Bbf"], V["Bf32"]
    OC = DC
    HF = KC * 64
    ph = ExitStack()
    psW = [ph.enter_context(_pt(nc, "psW", [128, 1024], F32))
           for _ in range(2)]
    psSum = ph.enter_context(_pt(nc, "psSum", [1, 1024], F32))
    psSq = ph.enter_context(_pt(nc, "psSq", [1, 1024], F32))
    stage = [Bf32[:, 0:HF], Bf32[:, 2048:2048 + HF]]
    wbuf = [Bbf[:, 8192:8192 + KC * 128], Bbf[:, 12288:12288 + KC * 128]]
    rslab = Bbf[:, 16384:17408]
    xslab = Bbf[:, 17408:18432]
    sqslab = Bbf[:, 18432:19456]
    meanB = Bbf[:, 19456:20480]
    rstdB = Bbf[:, 20480:21504]
    nslab = Bbf[:, 21504:22528]
    oslab_f = Bf32[:, 11264:12288]
    lnrows = sm["lnrows"]
    lnrow_h = sm["lnrow_h"]
    ones_c, ones_r = sm["ones_c"], sm["ones_r"]

    stg_free = [0, 0]
    wbuf_free = [0, 0]
    psW_free = [0, 0]
    wsem = ["wA", "wB"]
    for th in range(2):
        tw = slice(th * 1024, (th + 1) * 1024)
        rv = 0
        sv = 0
        st_pe = 0
        xv = 0
        sqv = 0
        for oc in range(OC):
            sl = oc % 2
            if not wbf16:
                for hf in range(2):
                    p.wait("sync", "ac", stg_free[hf])
                    wv = p.dma("sync", stage[hf],
                               w_hbm[oc][:, hf * HF:(hf + 1) * HF], wsem[hf])
                    p.wait("scalar", wsem[hf], wv)
                    if hf == 0:
                        p.wait("scalar", "pe", wbuf_free[sl])
                    stg_free[hf] = p.op(
                        "scalar", lambda e, s=stage[hf],
                        w=wbuf[sl][:, hf * HF:(hf + 1) * HF]:
                        e.copy(out=w, in_=s), "ac")
                cast_done = stg_free[1]
                p.wait("tensor", "ac", cast_done)
            else:
                p.wait("sync", "pe", wbuf_free[sl])
                wv = p.dma("sync", wbuf[sl], w_hbm[oc], wsem[sl])
                p.wait("tensor", wsem[sl], wv)
            if xold is not None:
                p.wait("gpsimd", "dv", xv)
                xv16 = p.dma("gpsimd", xslab, xold[oc][:, tw], "gA")
            p.wait("tensor", "dv", psW_free[sl])
            mmv = None
            for tt in range(2):
                for kc in range(KC):
                    def mm(e, o=psW[sl][:, tt * 512:(tt + 1) * 512],
                           w=wbuf[sl][:, kc * 128:(kc + 1) * 128],
                           r=rhs_chunk(kc, slice(th * 1024 + tt * 512,
                                                 th * 1024 + tt * 512 + 512)),
                           st=(kc == 0), sp=(kc == KC - 1)):
                        return e.matmul(o, w, r, start=st, stop=sp,
                                        skip_group_check=True)
                    last = (tt == 1) and (kc == KC - 1)
                    mmv = p.op("tensor", mm, "pe" if last else None)
            wbuf_free[sl] = mmv
            p.wait("vector", "pe", mmv)
            p.wait("vector", "pe", st_pe)
            p.wait("vector", "ac", sqv)
            p.wait("vector", "sA", sv)
            if xold is not None:
                p.wait("vector", "gA", xv16)
                xv = None
                xin = xslab
            else:
                xin = rhs_chunk(oc, tw) if KC == DC else None
                assert xin is not None
            if bias_cols is not None:
                rv = p.op("vector", lambda e, sps=psW[sl], xin=xin, oc=oc,
                          d=rslab: e.scalar_tensor_tensor(
                              out=d, in0=sps[:],
                              scalar=bias_cols[:, oc:oc + 1],
                              in1=xin, op0=ALU.add, op1=ALU.add), "dv")
            else:
                rv = p.op("vector", lambda e, sps=psW[sl], xin=xin,
                          d=rslab: e.tensor_add(out=d, in0=sps[:],
                                                in1=xin), "dv")
            psW_free[sl] = rv
            if xold is not None:
                xv = rv
            p.wait("gpsimd", "dv", rv)
            sv = p.dma("gpsimd", resid_scr[oc][:, tw], rslab, "sA")
            p.wait("scalar", "dv", rv)
            p.wait("scalar", "pe", st_pe)
            sqv = p.op("scalar", lambda e, s=rslab, d=sqslab:
                       e.activation(out=d, in_=s, func=AF.Square), "ac")
            p.wait("tensor", "dv", rv)
            p.wait("tensor", "ac", sqv)
            for half in range(2):
                hs = slice(half * 512, (half + 1) * 512)
                p.op("tensor", lambda e, hs=hs, s=rslab, oc=oc:
                     e.matmul(psSum[:, hs], ones_c[:], s[:, hs],
                              start=(oc == 0), stop=(oc == OC - 1),
                              skip_group_check=True), None)
                st_pe = p.op(
                    "tensor", lambda e, hs=hs, s=sqslab, oc=oc:
                    e.matmul(psSq[:, hs], ones_c[:], s[:, hs],
                             start=(oc == 0), stop=(oc == OC - 1),
                             skip_group_check=True),
                    "pe" if half == 1 else None)
        p.wait("vector", "pe", st_pe)
        p.wait("scalar", "pe", st_pe)
        mh = p.op("vector", lambda e, th=th: e.tensor_scalar(
            out=lnrow_h[0:1, th * 2048:th * 2048 + 1024], in0=psSum[:],
            scalar1=1.0 / D, scalar2=None, op0=ALU.mult), "dv")
        v1 = p.op("vector", lambda e: e.tensor_scalar(
            out=lnrows[0:1, 0:1024], in0=psSq[:], scalar1=1.0 / D,
            scalar2=None, op0=ALU.mult), "dv")
        p.wait("scalar", "dv", mh)
        v2 = p.op("scalar", lambda e: e.activation(
            out=psSum[:], in_=psSum[:], func=AF.Square, scale=1.0 / D), "ac")
        p.wait("vector", "ac", v2)
        p.wait("vector", "dv", v1)
        v3 = p.op("vector", lambda e: e.scalar_tensor_tensor(
            out=lnrows[0:1, 0:1024], in0=lnrows[0:1, 0:1024], scalar=EPS,
            in1=psSum[:], op0=ALU.add, op1=ALU.subtract), "dv")
        p.wait("scalar", "dv", v3)
        v4 = p.op("scalar", lambda e: e.activation(
            out=psSum[:], in_=lnrows[0:1, 0:1024], func=AF.Sqrt), "ac")
        p.wait("vector", "ac", v4)
        v5 = p.op("vector", lambda e: e.reciprocal(
            out=lnrows[0:1, 0:1024], in_=psSum[:]), "dv")
        p.wait("vector", "dv", v5)
        p.op("vector", lambda e, th=th: e.tensor_copy(
            out=lnrow_h[0:1, th * 2048 + 1024:th * 2048 + 2048],
            in_=lnrows[0:1, 0:1024]), "dv")
        p.barrier()
    for th in range(2):
        tw = slice(th * 1024, (th + 1) * 1024)
        for which, dstB in [(0, meanB), (1, rstdB)]:
            base = th * 2048 + which * 1024
            p.op("tensor", lambda e, base=base: e.matmul(
                psW[0][:, 0:512], ones_r[:],
                lnrow_h[0:1, base:base + 512], start=True, stop=True,
                skip_group_check=True), None)
            bv = p.op("tensor", lambda e, base=base: e.matmul(
                psW[0][:, 512:1024], ones_r[:],
                lnrow_h[0:1, base + 512:base + 1024], start=True, stop=True,
                skip_group_check=True), "pe")
            p.wait("scalar", "pe", bv)
            bv = p.op("scalar", lambda e, dstB=dstB: e.copy(
                out=dstB, in_=psW[0][:]), "ac")
            p.wait("tensor", "ac", bv)
        p.wait("vector", "ac", p.cnt["ac"])
        nv_rel = 0
        st_rel = 0
        for oc in range(DC):
            p.wait("gpsimd", "dv", nv_rel)
            p.wait("gpsimd", "sB", st_rel)
            nv = p.dma("gpsimd", nslab, resid_scr[oc][:, tw], "gC")
            p.wait("vector", "gC", nv)
            p.wait("vector", "sB", st_rel)
            osl = oslab_f if out_f32 else nslab
            vv = p.op("vector", lambda e, osl=osl: e.tensor_tensor(
                out=osl, in0=nslab, in1=meanB, op=ALU.subtract), "dv")
            p.wait("vector", "dv", vv)
            vv = p.op("vector", lambda e, osl=osl: e.tensor_tensor(
                out=osl, in0=osl, in1=rstdB, op=ALU.mult), "dv")
            nv_rel = vv
            p.wait("gpsimd", "dv", vv)
            if out_f32:
                st_rel = p.dma("gpsimd", out[oc * 128:(oc + 1) * 128, tw],
                               oslab_f, "sB")
            else:
                st_rel = p.dma("gpsimd", out[oc][:, tw], nslab, "sB")
        p.barrier()
    ph.close()


def ph_swiglu(p, nc, V, v1_b, b1s, p_chunks, act_scr):
    Abf, Bbf, Bf32 = V["Abf"], V["Bbf"], V["Bf32"]
    ph = ExitStack()
    psG = ph.enter_context(_pt(nc, "psG", [128, 2048], F32))
    psU = ph.enter_context(_pt(nc, "psU", [128, 2048], F32))
    PB = Abf[:, 0:16384]
    stage = [Bf32[:, 0:1024], Bf32[:, 1024:2048]]
    wbuf = [Bbf[:, 4096:5120], Bbf[:, 5120:6144]]
    silu_t = [Bbf[:, 6144:8192], Bbf[:, 8192:10240]]
    actsl = [Bbf[:, 10240:12288], Bbf[:, 12288:14336]]
    for kc in range(8):
        p.dma("sync", PB[:, kc * 2048:(kc + 1) * 2048], p_chunks[kc], "rA")
    p.barrier()
    wsem = ["wA", "wB"]
    ssem = ["sA", "sB"]
    stage_free = [0, 0]
    wbuf_free = [0, 0]
    silu_free = [0, 0]
    act_store = [0, 0]
    gate_rel = 0
    up_rel = 0
    for i in range(86):
        done = {}
        for gu in range(2):
            oc = i if gu == 0 else 86 + i
            sl = gu
            p.wait("sync", "ac", stage_free[sl])
            wv = p.dma("sync", stage[sl], v1_b[oc], wsem[sl])
            p.wait("scalar", wsem[sl], wv)
            p.wait("scalar", "pe", wbuf_free[sl])
            stage_free[sl] = p.op(
                "scalar", lambda e, s=stage[sl], w=wbuf[sl]: e.copy(
                    out=w, in_=s), "ac")
            p.wait("tensor", "ac", stage_free[sl])
            if gu == 0:
                p.wait("tensor", "ac", gate_rel)
            else:
                p.wait("tensor", "dv", up_rel)
            tgt = psG if gu == 0 else psU
            mmv = None
            for tt in range(4):
                for kc in range(8):
                    def mm(e, o=tgt[:, tt * 512:(tt + 1) * 512],
                           w=wbuf[sl][:, kc * 128:(kc + 1) * 128],
                           r=PB[:, kc * 2048 + tt * 512:
                                kc * 2048 + tt * 512 + 512],
                           st=(kc == 0), sp=(kc == 7)):
                        return e.matmul(o, w, r, start=st, stop=sp,
                                        skip_group_check=True)
                    last = (tt == 3) and (kc == 7)
                    mmv = p.op("tensor", mm, "pe" if last else None)
            done[gu] = mmv
            wbuf_free[sl] = mmv
        ssl = i % 2
        p.wait("scalar", "pe", done[0])
        p.wait("scalar", "dv", silu_free[ssl])
        gate_rel = p.op("scalar", lambda e, i=i, ssl=ssl: e.activation(
            out=silu_t[ssl], in_=psG[:], func=AF.Silu,
            bias=b1s[:, i:i + 1]), "ac")
        p.wait("vector", "pe", done[1])
        p.wait("vector", "ac", gate_rel)
        p.wait("vector", ssem[ssl], act_store[ssl])
        up_rel = p.op("vector", lambda e, i=i, ssl=ssl: e.scalar_tensor_tensor(
            out=actsl[ssl], in0=psU[:], scalar=b1s[:, 86 + i:86 + i + 1],
            in1=silu_t[ssl], op0=ALU.add, op1=ALU.mult), "dv")
        silu_free[ssl] = up_rel
        p.wait("gpsimd", "dv", up_rel)
        for tt in range(4):
            act_store[ssl] = p.dma("gpsimd", act_scr[i][tt],
                                   actsl[ssl][:, tt * 512:(tt + 1) * 512],
                                   ssem[ssl])
    ph.close()


def ph_z(p, nc, V, u2_b, act_scr, z_scr):
    Abf, Bbf, Bf32 = V["Abf"], V["Bbf"], V["Bf32"]
    ph = ExitStack()
    psZ2 = [ph.enter_context(_pt(nc, "psZ2", [128, 2048], F32))
            for _ in range(2)]
    U2H = Abf[:, 0:86 * 512]
    stage = [Bf32[:, 0:512], Bf32[:, 512:1024]]
    aslab = [Bbf[:, 2048 + i * 512:2048 + (i + 1) * 512] for i in range(4)]
    zslab = [Bbf[:, 4096:4608], Bbf[:, 4608:5120]]
    wsem = ["wA", "wB"]
    a_pe = [0, 0, 0, 0]
    z_store = [0, 0]
    ps_rel = [0, 0]
    stage_free = [0, 0]
    for rh in range(2):
        if rh == 1:
            p.wait("scalar", "pe", p.cnt["pe"])
        for kc in range(86):
            sl = kc % 2
            p.wait("sync", "ac", stage_free[sl])
            wv = p.dma("sync", stage[sl], u2_b[rh, kc], wsem[sl])
            p.wait("scalar", wsem[sl], wv)
            stage_free[sl] = p.op(
                "scalar", lambda e, s=stage[sl], kc=kc: e.copy(
                    out=U2H[:, kc * 512:(kc + 1) * 512], in_=s), "ac")
        cast_all = p.cnt["ac"]
        p.wait("tensor", "ac", cast_all)
        for tt in range(4):
            psl = tt % 2
            p.wait("tensor", "dv", ps_rel[psl])
            for kc in range(86):
                asl = kc % 4
                qeng = "sync" if (kc % 2 == 0) else "gpsimd"
                qsem = ("qA", "gA", "qB", "gB")[asl]
                p.wait(qeng, "pe", a_pe[asl])
                av = p.dma(qeng, aslab[asl], act_scr[kc][tt], qsem)
                p.wait("tensor", qsem, av)
                for rc in range(4):
                    def mm(e, o=psZ2[psl][:, rc * 512:(rc + 1) * 512],
                           w=U2H[:, kc * 512 + rc * 128:
                                 kc * 512 + rc * 128 + 128],
                           r=aslab[asl], st=(kc == 0), sp=(kc == 85)):
                        return e.matmul(o, w, r, start=st, stop=sp,
                                        skip_group_check=True)
                    a_pe[asl] = p.op("tensor", mm, "pe" if rc == 3 else None)
            mm_done = p.cnt["pe"]
            p.wait("vector", "pe", mm_done)
            for rc in range(4):
                zsl = rc % 2
                p.wait("vector", ("sA", "sB")[zsl], z_store[zsl])
                zv = p.op("vector", lambda e, rc=rc, psl=psl, zsl=zsl:
                          e.tensor_copy(
                              out=zslab[zsl],
                              in_=psZ2[psl][:, rc * 512:(rc + 1) * 512]),
                          "dv")
                p.wait("gpsimd", "dv", zv)
                z_store[zsl] = p.dma(
                    "gpsimd",
                    z_scr[rh * 4 + rc][:, tt * 512:(tt + 1) * 512],
                    zslab[zsl], ("sA", "sB")[zsl])
            ps_rel[psl] = p.cnt["dv"]
    ph.close()


def ph_dequant(p, nc, V, codesT, codes_bf, wb_dram, sm):
    Bbf, Bf32, Bi32 = V["Bbf"], V["Bf32"], V["Bi32"]
    ph = ExitStack()
    psBC = [ph.enter_context(_pt(nc, "psBC", [128, 512], F32))
            for _ in range(2)]
    psWd = ph.enter_context(_pt(nc, "psWd", [8, 512], F32))
    psSc = ph.enter_context(_pt(nc, "psSc", [8, 512], F32))
    ci32 = Bi32[:, 0:4096]
    cf32 = Bf32[:, 4096:8192]
    cbf = Bbf[:, 16384:20480]
    strip = Bbf[0:1, 1024:9216]
    oh = Bbf[:, 20480:22528]
    wline = Bbf[0:8, 22528:26624]
    iot, e01, cbf32 = sm["iot"], sm["e01"], sm["cbf32"]
    scalef, scaleB = sm["scalef"], sm["scaleB"]
    ones_r, ones8f = sm["ones_r"], sm["ones8f"]

    v = p.op("vector", lambda e: e.tensor_copy(out=e01[:], in_=cbf32[:]), "dv")
    sv = p.op("tensor", lambda e: e.matmul(
        psSc[:], ones8f[:], scalef[:], start=True, stop=True,
        skip_group_check=True), "pe")
    p.wait("vector", "pe", sv)
    sv = p.op("vector", lambda e: e.tensor_copy(out=scaleB[:],
                                                in_=psSc[:]), "dv")
    cl = p.dma("sync", ci32,
               codesT.rearrange("j c o -> (j c o)").rearrange(
                   "(q n) -> q n", q=128), "rA")
    p.wait("vector", "rA", cl)
    v1 = p.op("vector", lambda e: e.tensor_copy(out=cf32, in_=ci32), "dv")
    p.wait("vector", "dv", v1)
    v2 = p.op("vector", lambda e: e.tensor_copy(out=cbf, in_=cf32), "dv")
    p.wait("gpsimd", "dv", v2)
    v3 = p.dma("gpsimd",
               codes_bf.rearrange("j c o -> (j c o)").rearrange(
                   "(q n) -> q n", q=128), cbf, "sA")
    p.wait("sync", "sA", v3)
    p.barrier()

    strip_pe = 0
    oh_dv = 0
    wl_st = 0
    for jg in range(64):
        p.wait("sync", "pe", strip_pe)
        stv = p.dma("sync", strip,
                    codes_bf[jg * 8:(jg + 1) * 8].rearrange(
                        "j c o -> (j c o)")[None, :], "qA")
        p.wait("tensor", "qA", stv)
        for jj in range(8):
            for cb in range(2):
                p.wait("tensor", "dv", oh_dv)
                strip_pe = p.op(
                    "tensor", lambda e, jj=jj, cb=cb: e.matmul(
                        psBC[cb][:], ones_r[:],
                        strip[0:1, (jj * 2 + cb) * 512:
                              (jj * 2 + cb) * 512 + 512],
                        start=True, stop=True, skip_group_check=True), "pe")
            p.wait("vector", "pe", strip_pe)
            for cc in range(4):
                oh_dv = p.op(
                    "vector", lambda e, cc=cc:
                    e.tensor_scalar(out=oh[:, cc * 512:(cc + 1) * 512],
                                    in0=psBC[cc // 2][:],
                                    scalar1=iot[:, (cc % 2):(cc % 2) + 1],
                                    scalar2=None, op0=ALU.is_equal), "dv")
            p.wait("tensor", "dv", oh_dv)
            wdv = None
            for cc in range(4):
                wdv = p.op("tensor", lambda e, cc=cc: e.matmul(
                    psWd[:], e01[:, cc * 8:(cc + 1) * 8],
                    oh[:, cc * 512:(cc + 1) * 512],
                    start=(cc == 0), stop=(cc == 3), skip_group_check=True),
                    "pe" if cc == 3 else None)
            p.wait("vector", "pe", wdv)
            p.wait("vector", "gC", wl_st)
            wlv = p.op("vector", lambda e, jj=jj: e.tensor_tensor(
                out=wline[:, jj * 512:(jj + 1) * 512],
                in0=psWd[:], in1=scaleB[:], op=ALU.mult), "dv")
            p.wait("tensor", "dv", wlv)
        wl_done = p.cnt["dv"]
        p.wait("gpsimd", "dv", wl_done)
        wl_st = p.dma(
            "gpsimd",
            wb_dram[jg * 64:(jg + 1) * 64, :].rearrange(
                "(g j) o -> j g o", j=8),
            wline.rearrange("j (g o) -> j g o", g=8), "gC")
    ph.close()


def _smalls(nc, es, p, names):
    sm = {}
    mk = {
        "ident": ([128, 128], BF16), "ones_c": ([128, 1], BF16),
        "ones_r": ([1, 128], BF16), "negones_r": ([1, 128], BF16),
        "ones8f": ([1, 8], F32),
        "lnrows": ([1, 1024], F32), "lnrow_h": ([1, 4096], BF16),
        "cs_h": ([1, 1], BF16), "urow_h": ([1, 512], BF16),
        "iot": ([128, 2], F32), "e01": ([128, 32], BF16),
        "cbf32": ([128, 32], F32), "scalef": ([1, 512], F32),
        "scaleB": ([8, 512], F32), "bias32": ([128, 32], F32),
        "b1s": ([128, 172], F32),
    }
    for n in names:
        sm[n] = es.enter_context(
            nc.sbuf_tensor(f"{n}_{next(_uid)}", *mk[n]))[:]
    memsets = {"ones_c": 1.0, "ones_r": 1.0, "negones_r": -1.0, "ones8f": 1.0}
    mv = None
    for n in names:
        if n in memsets:
            mv = p.op("vector", lambda e, t=sm[n], v=memsets[n]:
                      e.memset(t[:], v), "dv")
    if mv is not None:
        p.barrier()
    return sm


def _din(nc, name, shape, dt=F32):
    return nc.dram_tensor(name, shape, dt, kind="ExternalInput").ap()


def _dout(nc, name, shape, dt=BF16):
    return nc.dram_tensor(name, shape, dt, kind="ExternalOutput").ap()


def build_qkv0():
    nc, es, p, V = _mk(131072, 32768)
    x_fm = _din(nc, "x_fm", [D, S])
    wq_b = _din(nc, "wq_b", [DC, 128, D])
    wk_b = _din(nc, "wk_b", [DC, 128, D])
    wv_b = _din(nc, "wv_b", [DC, 128, D])
    xbf = _dout(nc, "xbf", [DC, 128, S])
    q_scr = _dout(nc, "q_scr", [NH, 128, S])
    k_scr = _dout(nc, "k_scr", [NH, 128, S])
    v_scr = _dout(nc, "v_scr", [NH, 128, S])
    Abf = V["Abf"]

    def xch(kc, ts):
        return Abf[:, kc * S:(kc + 1) * S][:, ts]

    ph_load_x(p, V, x_fm, xbf)
    p.barrier()
    ph_proj(p, nc, V, wq_b, DC, NH, xch, q_scr, scale=INVSQ)
    p.barrier()
    ph_proj(p, nc, V, wk_b, DC, NH, xch, k_scr)
    p.barrier()
    ph_proj(p, nc, V, wv_b, DC, NH, xch, v_scr)
    p.barrier()
    p.emit()
    return nc, es


def build_attn(layer):
    nc, es, p, V = _mk(16384, 54272)
    sm = _smalls(nc, es, p, ["ident", "ones_c", "ones_r", "negones_r",
                             "cs_h", "urow_h"])
    cD_in = _din(nc, "cD_in", [128, S], BF16)
    sD_in = _din(nc, "sD_in", [128, S], BF16)
    ident_in = _din(nc, "ident_in", [128, 128], BF16)
    attn_scr = _dout(nc, "attn_scr", [DC, 128, S])
    Abf = V["Abf"]
    p.dma("sync", Abf[:, 0:2048], cD_in, "rA")
    p.dma("sync", Abf[:, 2048:4096], sD_in, "rA")
    p.dma("sync", sm["ident"][:], ident_in, "rA")
    if layer == 0:
        q_scr = _din(nc, "q_scr", [NH, 128, S], BF16)
        k_scr = _din(nc, "k_scr", [NH, 128, S], BF16)
        v_scr = _din(nc, "v_scr", [NH, 128, S], BF16)
        heads = [(h, q_scr[h], k_scr[h], v_scr[h], h, h, False)
                 for h in range(NH)]
    else:
        masks_in = _din(nc, "masks_in", [128, S], BF16)
        p.dma("sync", Abf[:, 4096:6144], masks_in, "rA")
        qkv_scr = _din(nc, "qkv_scr", [48, 128, S], BF16)
        heads = [(4 * hk + g, qkv_scr[4 * hk + g], qkv_scr[32 + hk],
                  qkv_scr[40 + hk], 4 * hk + g, hk, True)
                 for hk in range(8) for g in range(4)]
    p.barrier()
    ph_attention(p, nc, V, layer, heads, attn_scr, sm)
    p.barrier()
    p.emit()
    return nc, es


def build_resid_ln(name_w, KC, rhs_name, rhs_nch, xold_name, out_f32,
                   bias=False, wbf16=False):
    nc, es, p, V = _mk(131072, 45056)
    names = ["ones_c", "ones_r", "lnrows", "lnrow_h"]
    if bias:
        names.append("bias32")
    sm = _smalls(nc, es, p, names)
    w_hbm = _din(nc, name_w, [DC, 128, KC * 128], BF16 if wbf16 else F32)
    rhs = _din(nc, rhs_name, [rhs_nch, 128, S], BF16)
    xold = _din(nc, xold_name, [DC, 128, S], BF16) if xold_name else None
    bias_t = _din(nc, "bias_t", [128, DC]) if bias else None
    resid_scr = nc.dram_tensor(f"resid_{next(_uid)}", [DC, 128, S],
                               BF16).ap()
    if out_f32:
        out = nc.dram_tensor("y", [D, S], F32, kind="ExternalOutput").ap()
    else:
        out = _dout(nc, "xout", [DC, 128, S])
    Abf = V["Abf"]
    ph_load_A(p, Abf, [rhs[c] for c in range(rhs_nch)])
    if bias:
        p.dma("sync", sm["bias32"][:], bias_t, "rA")
    p.barrier()

    def rhs_chunk(kc, ts):
        return Abf[:, kc * S:(kc + 1) * S][:, ts]

    ph_resid_ln(p, nc, V, w_hbm, KC, rhs_chunk, xold, out, out_f32, sm,
                bias_cols=sm["bias32"] if bias else None, wbf16=wbf16,
                resid_scr=resid_scr)
    p.barrier()
    p.emit()
    return nc, es


def build_p0():
    nc, es, p, V = _mk(131072, 32768)
    x1 = _din(nc, "x1", [DC, 128, S], BF16)
    u1_b = _din(nc, "u1_b", [8, 128, D])
    p_scr = _dout(nc, "p_scr", [8, 128, S])
    Abf = V["Abf"]
    ph_load_A(p, Abf, [x1[c] for c in range(DC)])
    p.barrier()

    def xch(kc, ts):
        return Abf[:, kc * S:(kc + 1) * S][:, ts]

    ph_proj(p, nc, V, u1_b, DC, 8, xch, p_scr)
    p.barrier()
    p.emit()
    return nc, es


def build_swiglu():
    nc, es, p, V = _mk(32768, 16384)
    sm = _smalls(nc, es, p, ["b1s"])
    p_scr = _din(nc, "p_scr", [8, 128, S], BF16)
    v1_b = _din(nc, "v1_b", [172, 128, 1024])
    b1T = _din(nc, "b1T", [128, 172])
    act_scr = _dout(nc, "act_scr", [86, 4, 128, 512])
    p.dma("sync", sm["b1s"][:], b1T, "rA")
    p.barrier()
    ph_swiglu(p, nc, V, v1_b, sm["b1s"], [p_scr[i] for i in range(8)],
              act_scr)
    p.barrier()
    p.emit()
    return nc, es


def build_z():
    nc, es, p, V = _mk(90112, 8192)
    act_scr = _din(nc, "act_scr", [86, 4, 128, 512], BF16)
    u2_b = _din(nc, "u2_b", [2, 86, 128, 512])
    z_scr = _dout(nc, "z_scr", [8, 128, S])
    ph_z(p, nc, V, u2_b, act_scr, z_scr)
    p.barrier()
    p.emit()
    return nc, es


def build_qkv1():
    nc, es, p, V = _mk(131072, 32768)
    x2 = _din(nc, "x2", [DC, 128, S], BF16)
    win_b = _din(nc, "win_b", [48, 128, D])
    qkv_scr = _dout(nc, "qkv_scr", [48, 128, S])
    Abf = V["Abf"]
    ph_load_A(p, Abf, [x2[c] for c in range(DC)])
    p.barrier()

    def xch(kc, ts):
        return Abf[:, kc * S:(kc + 1) * S][:, ts]

    ph_proj(p, nc, V, win_b, DC, 48, xch, qkv_scr)
    p.barrier()
    p.emit()
    return nc, es


def build_dequant():
    nc, es, p, V = _mk(0, 61440)
    sm = _smalls(nc, es, p, ["iot", "e01", "cbf32", "scalef", "scaleB",
                             "ones_r", "ones8f"])
    codesT = _din(nc, "codesT", [512, 2, 512], I32)
    cbE = _din(nc, "cbE", [128, 32])
    myscale = _din(nc, "myscale", [1, 512])
    iot_in = _din(nc, "iot_in", [128, 2])
    wb_dram = _dout(nc, "wb_dram", [4096, 512])
    codes_bf = nc.dram_tensor("codes_bf", [512, 2, 512], BF16).ap()
    p.dma("sync", sm["cbf32"][:], cbE, "rA")
    p.dma("sync", sm["scalef"][:], myscale, "rA")
    p.dma("sync", sm["iot"][:], iot_in, "rA")
    p.barrier()
    ph_dequant(p, nc, V, codesT, codes_bf, wb_dram, sm)
    p.barrier()
    p.emit()
    return nc, es


def build_fused(mock_cc=False):
    nc, es, p, V = _mk(131072, 61440)
    Abf, Bbf, Bf32 = V["Abf"], V["Bbf"], V["Bf32"]
    sm = _smalls(nc, es, p, ["ident", "ones_c", "ones_r", "negones_r",
                             "ones8f", "cs_h", "urow_h", "iot", "e01",
                             "cbf32", "bias32", "b1s"])
    smR = dict(sm)
    smR["lnrows"] = Bf32[0:1, 12288:13312]
    smR["lnrow_h"] = Bbf[0:1, 26624:30720]
    smD = dict(sm)
    smD["scalef"] = Bf32[0:1, 13312:13824]
    smD["scaleB"] = Bf32[0:8, 13824:14336]

    x_fm = _din(nc, "x_fm", [D, S])
    wq_b = _din(nc, "wq_b", [DC, 128, D])
    wk_b = _din(nc, "wk_b", [DC, 128, D])
    wv_b = _din(nc, "wv_b", [DC, 128, D])
    wo_b = _din(nc, "wo_b", [DC, 128, D])
    u1_b = _din(nc, "u1_b", [8, 128, D])
    v1_b = _din(nc, "v1_b", [172, 128, 1024])
    u2_b = _din(nc, "u2_b", [2, 86, 128, 512])
    v2_b = _din(nc, "v2_b", [DC, 128, 1024])
    win_b = _din(nc, "win_b", [48, 128, D])
    wout_b = _din(nc, "wout_b", [DC, 128, D])
    b1T = _din(nc, "b1T", [128, 172])
    b2T = _din(nc, "b2T", [128, DC])
    bL1T = _din(nc, "bL1T", [128, DC])
    codesT = _din(nc, "codesT", [512, 2, 512], I32)
    cbE = _din(nc, "cbE", [128, 32])
    myscale = _din(nc, "myscale", [1, 512])
    iot_in = _din(nc, "iot_in", [128, 2])
    cD_in = _din(nc, "cD_in", [128, S], BF16)
    sD_in = _din(nc, "sD_in", [128, S], BF16)
    ident_in = _din(nc, "ident_in", [128, 128], BF16)
    masks_in = _din(nc, "masks_in", [128, S], BF16)
    y = nc.dram_tensor("y", [D, S], F32, kind="ExternalOutput").ap()

    def scr(name, shape, dt=BF16):
        return nc.dram_tensor(name, shape, dt).ap()

    xbf = scr("xbf", [DC, 128, S])
    q_scr = scr("q_scr", [NH, 128, S])
    k_scr = scr("k_scr", [NH, 128, S])
    v_scr = scr("v_scr", [NH, 128, S])
    attn_scr = scr("attn_scr", [DC, 128, S])
    x1 = scr("x1", [DC, 128, S])
    p_scr = scr("p_scr", [8, 128, S])
    act_scr = scr("act_scr", [86, 4, 128, 512])
    z_scr = scr("z_scr", [8, 128, S])
    x2 = scr("x2", [DC, 128, S])
    qkv_scr = scr("qkv_scr", [48, 128, S])
    x3 = scr("x3", [DC, 128, S])
    resid_scr = scr("resid_scr", [DC, 128, S])
    codes_bf = scr("codes_bf", [512, 2, 512])
    wb_dram = scr("wb_dram", [4096, 512])
    wfull = scr("wfull", [8 * 4096, 512])
    wfin4 = scr("wfin4", [DC, 128, 32, 128])

    def xch(kc, ts):
        return Abf[:, kc * S:(kc + 1) * S][:, ts]

    p.dma("sync", sm["ident"][:], ident_in, "rA")
    p.dma("sync", sm["iot"][:], iot_in, "rA")
    p.dma("sync", sm["cbf32"][:], cbE, "rA")
    p.dma("sync", sm["b1s"][:], b1T, "rA")
    p.dma("sync", smD["scalef"][:], myscale, "rA")
    p.barrier()

    ph_dequant(p, nc, V, codesT, codes_bf, wb_dram, smD)
    p.barrier()
    if mock_cc:
        for r_ in range(8):
            p.dma("gpsimd", wfull[r_ * 4096:(r_ + 1) * 4096, :], wb_dram,
                  "cc")
        p.wait("gpsimd", "cc", p.cnt["cc"])
    else:
        p.op("gpsimd", lambda e: e.collective_compute(
            "AllGather", ALU.bypass, replica_groups=[list(range(8))],
            ins=[wb_dram], outs=[wfull]), "cc")
        p.wait("gpsimd", "cc", 1)
    p.wait("scalar", "cc", p.cnt["cc"] if mock_cc else 1)
    for oc in range(DC):
        r = oc // 4
        csl = slice((oc % 4) * 128, (oc % 4) * 128 + 128)
        for qtr in range(4):
            src = wfull[r * 4096 + qtr * 1024:r * 4096 + (qtr + 1) * 1024,
                        csl].rearrange("(a q) m -> q a m", q=128)
            p.dma("scalar", wfin4[oc][:, qtr * 8:(qtr + 1) * 8, :], src, "rB")
    p.barrier()

    ph_load_x(p, V, x_fm, xbf)
    p.barrier()
    ph_proj(p, nc, V, wq_b, DC, NH, xch, q_scr, scale=INVSQ)
    p.barrier()
    ph_proj(p, nc, V, wk_b, DC, NH, xch, k_scr)
    p.barrier()
    ph_proj(p, nc, V, wv_b, DC, NH, xch, v_scr)
    p.barrier()

    p.dma("sync", Abf[:, 0:2048], cD_in, "rA")
    p.dma("sync", Abf[:, 2048:4096], sD_in, "rA")
    p.barrier()
    heads0 = [(h, q_scr[h], k_scr[h], v_scr[h], h, h, False)
              for h in range(NH)]
    ph_attention(p, nc, V, 0, heads0, attn_scr, sm)
    p.barrier()

    ph_load_A(p, Abf, [attn_scr[c] for c in range(DC)])
    p.barrier()
    ph_resid_ln(p, nc, V, wo_b, DC, xch, xbf, x1, False, smR,
                resid_scr=resid_scr)
    p.barrier()

    ph_load_A(p, Abf, [x1[c] for c in range(DC)])
    p.barrier()
    ph_proj(p, nc, V, u1_b, DC, 8, xch, p_scr)
    p.barrier()
    ph_swiglu(p, nc, V, v1_b, sm["b1s"], [p_scr[i] for i in range(8)],
              act_scr)
    p.barrier()
    ph_z(p, nc, V, u2_b, act_scr, z_scr)
    p.barrier()
    ph_load_A(p, Abf, [z_scr[c] for c in range(8)])
    p.dma("sync", sm["bias32"][:], b2T, "rA")
    p.barrier()
    ph_resid_ln(p, nc, V, v2_b, 8, xch, x1, x2, False, smR,
                bias_cols=sm["bias32"], resid_scr=resid_scr)
    p.barrier()

    ph_load_A(p, Abf, [x2[c] for c in range(DC)])
    p.barrier()
    ph_proj(p, nc, V, win_b, DC, 48, xch, qkv_scr)
    p.barrier()

    p.dma("sync", Abf[:, 0:2048], cD_in, "rA")
    p.dma("sync", Abf[:, 2048:4096], sD_in, "rA")
    p.dma("sync", Abf[:, 4096:6144], masks_in, "rA")
    p.barrier()
    heads1 = [(4 * hk + g, qkv_scr[4 * hk + g], qkv_scr[32 + hk],
               qkv_scr[40 + hk], 4 * hk + g, hk, True)
              for hk in range(8) for g in range(4)]
    ph_attention(p, nc, V, 1, heads1, attn_scr, sm)
    p.barrier()

    ph_load_A(p, Abf, [attn_scr[c] for c in range(DC)])
    p.barrier()
    ph_resid_ln(p, nc, V, wout_b, DC, xch, x2, x3, False, smR,
                resid_scr=resid_scr)
    p.barrier()

    ph_load_A(p, Abf, [x3[c] for c in range(DC)])
    p.dma("sync", sm["bias32"][:], bL1T, "rA")
    p.barrier()
    wfin_v = [wfin4[oc].rearrange("q a m -> q (a m)") for oc in range(DC)]
    ph_resid_ln(p, nc, V, wfin_v, DC, xch, None, y, True, smR,
                bias_cols=sm["bias32"], wbf16=True, resid_scr=resid_scr)
    p.barrier()
    p.emit()
    return nc, es


FUSED = True


def kernel_fused(inputs):
    sh, percore = _prep(**inputs)
    keys = ["wq_b", "wk_b", "wv_b", "wo_b", "u1_b", "v1_b", "u2_b", "v2_b",
            "win_b", "wout_b", "b1T", "b2T", "bL1T", "cbE", "iot_in",
            "cD_in", "sD_in", "ident_in", "masks_in"]
    in_maps = _maps(sh, percore, keys, ["x_fm", "codesT", "myscale"])
    res = _run(build_fused, in_maps)
    out = np.empty((8, S, D), np.float32)
    for r in range(8):
        out[r] = res[r]["y"].T
    return out


def _blk(W_io, OC):
    KIN = W_io.shape[0]
    KC = KIN // 128
    M = W_io.shape[1] // OC
    r = W_io.reshape(KC, 128, OC, M).transpose(2, 1, 0, 3)
    return np.ascontiguousarray(r.reshape(OC, 128, KC * M))


def _consts():
    inv = 1.0 / (10000.0 ** (np.arange(0, HD, 2) / HD))
    t = np.arange(S, dtype=np.float64)
    ang = np.outer(t, inv)
    c = np.cos(ang).astype(np.float32).T
    s = np.sin(ang).astype(np.float32).T
    cd = np.concatenate([c, c], 0).astype(BF)
    sd = np.concatenate([-s, s], 0).astype(BF)
    ident = np.eye(128, dtype=BF)
    masksv = np.zeros((128, 2048), np.float32)
    for mi, delta in enumerate([0, 128, 256, 384]):
        pp = np.arange(128)[:, None]
        ff = np.arange(512)[None, :]
        masksv[:, mi * 512:(mi + 1) * 512] = (pp + delta <= ff)
    iotv = np.stack([np.arange(128, dtype=np.float32),
                     np.arange(128, 256, dtype=np.float32)], 1)
    return cd, sd, ident, np.ascontiguousarray(masksv.astype(BF)), iotv


def _prep(L0_attn_x, L0_attn_Wq, L0_attn_Wk, L0_attn_Wv, L0_attn_Wo,
          L0_ffn_U1, L0_ffn_V1, L0_ffn_U2, L0_ffn_V2, L0_ffn_b1, L0_ffn_b2,
          L1_attn_w_in, L1_attn_w_out, L1_ffn_codes, L1_ffn_codebooks,
          L1_ffn_scales, L1_ffn_bias):
    f = np.asarray
    wq = f(L0_attn_Wq, np.float32)
    wk = f(L0_attn_Wk, np.float32)
    wv = f(L0_attn_Wv, np.float32)
    wo = f(L0_attn_Wo, np.float32)
    u1 = f(L0_ffn_U1, np.float32)
    v1 = f(L0_ffn_V1, np.float32)
    u2 = f(L0_ffn_U2, np.float32)
    v2 = f(L0_ffn_V2, np.float32)
    b1 = f(L0_ffn_b1, np.float32)
    b2 = f(L0_ffn_b2, np.float32)
    win = f(L1_attn_w_in, np.float32)
    wout = f(L1_attn_w_out, np.float32)
    codes = f(L1_ffn_codes, np.int32)
    cb = f(L1_ffn_codebooks, np.float32)
    scales = f(L1_ffn_scales, np.float32)
    bias = f(L1_ffn_bias, np.float32)
    x = f(L0_attn_x, np.float32)

    cd, sd, identv, masksv, iotv = _consts()
    sh = dict(
        wq_b=_blk(wq.T, DC), wk_b=_blk(wk.T, DC), wv_b=_blk(wv.T, DC),
        wo_b=_blk(wo.T, DC), u1_b=_blk(u1, 8), v1_b=_blk(v1, 172),
        u2_b=np.ascontiguousarray(
            u2.reshape(86, 128, 2, 512).transpose(2, 0, 1, 3)),
        v2_b=_blk(v2, DC), win_b=_blk(win.T, 48), wout_b=_blk(wout.T, DC),
        b1T=np.ascontiguousarray(b1.reshape(172, 128).T),
        b2T=np.ascontiguousarray(b2.reshape(32, 128).T),
        bL1T=np.ascontiguousarray(bias.reshape(32, 128).T),
        cbE=np.ascontiguousarray(cb.reshape(512, 8).reshape(4, 128, 8)
                                 .transpose(1, 0, 2).reshape(128, 32)),
        cD_in=cd, sD_in=sd, ident_in=identv, masks_in=masksv, iot_in=iotv,
    )
    percore = []
    for r in range(8):
        m = {}
        m["x_fm"] = np.ascontiguousarray(x[r].T)
        mc = codes[512 * r:512 * (r + 1)]
        m["codesT"] = np.ascontiguousarray(mc.transpose(1, 2, 0))
        m["myscale"] = np.ascontiguousarray(
            scales.reshape(-1)[512 * r:512 * (r + 1)].reshape(1, 512))
        percore.append(m)
    return sh, percore


def _run(builder, in_maps):
    nc, es = builder()
    res = run_bass_kernel_spmd(nc, in_maps, core_ids=list(range(8)))
    es.close()
    return res.results


def _maps(sh, percore, keys, pc_keys=()):
    out = []
    for r in range(8):
        m = {k: sh[k] for k in keys}
        for k in pc_keys:
            m[k] = percore[r][k]
        out.append(m)
    return out


def kernel(**inputs):
    if FUSED:
        return kernel_fused(inputs)
    sh, percore = _prep(**inputs)

    r1 = _run(build_qkv0,
              _maps(sh, percore, ["wq_b", "wk_b", "wv_b"], ["x_fm"]))
    r2 = _run(lambda: build_attn(0),
              [dict(cD_in=sh["cD_in"], sD_in=sh["sD_in"],
                    ident_in=sh["ident_in"], q_scr=r1[r]["q_scr"],
                    k_scr=r1[r]["k_scr"], v_scr=r1[r]["v_scr"])
               for r in range(8)])
    r3 = _run(lambda: build_resid_ln("wo_b", DC, "attn_scr", DC, "xbf",
                                     False),
              [dict(wo_b=sh["wo_b"], attn_scr=r2[r]["attn_scr"],
                    xbf=r1[r]["xbf"]) for r in range(8)])
    x1 = [r3[r]["xout"] for r in range(8)]
    r4 = _run(build_p0,
              [dict(x1=x1[r], u1_b=sh["u1_b"]) for r in range(8)])
    r5 = _run(build_swiglu,
              [dict(p_scr=r4[r]["p_scr"], v1_b=sh["v1_b"], b1T=sh["b1T"])
               for r in range(8)])
    r6 = _run(build_z,
              [dict(act_scr=r5[r]["act_scr"], u2_b=sh["u2_b"])
               for r in range(8)])
    r7 = _run(lambda: build_resid_ln("v2_b", 8, "z_scr", 8, "x1b",
                                     False, bias=True),
              [dict(v2_b=sh["v2_b"], z_scr=r6[r]["z_scr"], x1b=x1[r],
                    bias_t=sh["b2T"]) for r in range(8)])
    x2 = [r7[r]["xout"] for r in range(8)]
    r8 = _run(build_qkv1,
              [dict(x2=x2[r], win_b=sh["win_b"]) for r in range(8)])
    r9 = _run(lambda: build_attn(1),
              [dict(cD_in=sh["cD_in"], sD_in=sh["sD_in"],
                    ident_in=sh["ident_in"], masks_in=sh["masks_in"],
                    qkv_scr=r8[r]["qkv_scr"]) for r in range(8)])
    r10 = _run(lambda: build_resid_ln("wout_b", DC, "attn_scr", DC, "x2b",
                                      False),
               [dict(wout_b=sh["wout_b"], attn_scr=r9[r]["attn_scr"],
                     x2b=x2[r]) for r in range(8)])
    x3 = [r10[r]["xout"] for r in range(8)]
    r11 = _run(build_dequant,
               _maps(sh, percore, ["cbE", "iot_in"],
                     ["codesT", "myscale"]))
    WT = np.concatenate([r11[r]["wb_dram"] for r in range(8)], axis=1)
    wfin = _blk(WT, DC)
    r12 = _run(lambda: build_resid_ln("wfin", DC, "x3", DC, None,
                                      True, bias=True, wbf16=True),
               [dict(wfin=wfin, x3=x3[r], bias_t=sh["bL1T"])
                for r in range(8)])
    out = np.empty((8, S, D), np.float32)
    for r in range(8):
        out[r] = r12[r]["y"].T
    return out


# version 05
# speedup vs baseline: 1.3572x; 1.0766x over previous
import math
import itertools
import numpy as np
import ml_dtypes
from contextlib import ExitStack

import concourse.bass as bass
import concourse.mybir as mybir
from concourse.bass_utils import run_bass_kernel_spmd

F32 = mybir.dt.float32
BF16 = mybir.dt.bfloat16
I32 = mybir.dt.int32
AF = mybir.ActivationFunctionType
ALU = mybir.AluOpType
AX = mybir.AxisListType
BF = ml_dtypes.bfloat16

S = 2048
D = 4096
DC = 32
HD = 128
NH = 32
EPS = 1e-5
INVSQ = 1.0 / math.sqrt(128.0)

ENGS = ("sync", "vector", "tensor", "scalar", "gpsimd")
SEMS = ("wA", "wB", "rA", "rB", "sA", "sB", "qA", "qB",
        "gA", "gB", "gC", "gD", "pe", "dv", "ac", "gp", "cc")

_uid = itertools.count()


class P:
    def __init__(self, nc, es):
        self.nc = nc
        self.ops = {e: [] for e in ENGS}
        self.sems = {}
        self.cnt = {}
        for s in SEMS:
            self.sems[s] = es.enter_context(nc.semaphore(s))
            self.cnt[s] = 0

    def wait(self, eng, sem, val):
        if val is None or val <= 0:
            return
        s = self.sems[sem]
        self.ops[eng].append(lambda e, s=s, v=val: e.wait_ge(s, v))

    def dma(self, eng, out, in_, sem):
        s = self.sems[sem]
        self.cnt[sem] += 16
        self.ops[eng].append(
            lambda e, o=out, i=in_, s=s: e.dma_start(out=o, in_=i).then_inc(s, 16))
        return self.cnt[sem]

    def op(self, eng, fn, sem=None):
        if sem is None:
            self.ops[eng].append(fn)
            return None
        s = self.sems[sem]
        self.cnt[sem] += 1
        self.ops[eng].append(lambda e, f=fn, s=s: f(e).then_inc(s, 1))
        return self.cnt[sem]

    def barrier(self):
        snap = dict(self.cnt)
        for eng in ENGS:
            for sem, v in snap.items():
                if v > 0:
                    self.wait(eng, sem, v)

    def emit(self):
        with self.nc.Block() as block:
            for eng in ENGS:
                def run(e, L=self.ops[eng]):
                    for f in L:
                        f(e)
                getattr(block, eng)(run)


def _pt(nc, name, shape, dt=F32):
    return nc.psum_tensor(f"{name}_{next(_uid)}", shape, dt)


def _mk(a_bytes, b_bytes):
    nc = bass.Bass("TRN2", target_bir_lowering=False)
    es = ExitStack()
    p = P(nc, es)
    V = {}
    if a_bytes:
        A = es.enter_context(nc.sbuf_tensor(f"AA_{next(_uid)}",
                                            [128, a_bytes], mybir.dt.uint8))
        V["Abf"] = A[:].bitcast(BF16)
        V["Af32"] = A[:].bitcast(F32)
    if b_bytes:
        Bt = es.enter_context(nc.sbuf_tensor(f"BB_{next(_uid)}",
                                             [128, b_bytes], mybir.dt.uint8))
        V["Bbf"] = Bt[:].bitcast(BF16)
        V["Bf32"] = Bt[:].bitcast(F32)
        V["Bi32"] = Bt[:].bitcast(I32)
    return nc, es, p, V


def ph_load_A(p, Abf, srcs):
    for c, src in enumerate(srcs):
        p.dma("sync", Abf[:, c * S:(c + 1) * S], src, "rA")


def ph_load_x(p, V, x_fm, xbf):
    Abf, Bf32 = V["Abf"], V["Bf32"]
    stage = Bf32[:, 0:2048]
    cv = 0
    for c in range(DC):
        p.wait("sync", "ac", cv)
        wv = p.dma("sync", stage, x_fm[c * 128:(c + 1) * 128, :], "wA")
        p.wait("scalar", "wA", wv)
        dst = Abf[:, c * S:(c + 1) * S]
        cv = p.op("scalar", lambda e, s=stage, d=dst: e.copy(out=d, in_=s), "ac")
        p.wait("gpsimd", "ac", cv)
        p.dma("gpsimd", xbf[c], dst, "sA")


def ph_proj(p, nc, V, w_hbm, KC, OC, rhs_chunk, dest, scale=None):
    Bbf, Bf32 = V["Bbf"], V["Bf32"]
    ph = ExitStack()
    ps = [ph.enter_context(_pt(nc, "pj", [128, 2048], F32)) for _ in range(2)]
    stage = [Bf32[:, 0:KC * 128], Bf32[:, 4096:4096 + KC * 128]]
    wbuf = [Bbf[:, 16384:16384 + KC * 128], Bbf[:, 20480:20480 + KC * 128]]
    slabs = [Bbf[:, 24576:26624], Bbf[:, 26624:28672]]
    wsem = ["wA", "wB"]
    ssem = ["sA", "sB"]
    stage_free = [0, 0]
    wbuf_free = [0, 0]
    ps_free = [0, 0]
    slab_free = [0, 0]
    for oc in range(OC):
        sl = oc % 2
        p.wait("sync", "ac", stage_free[sl])
        wv = p.dma("sync", stage[sl], w_hbm[oc], wsem[sl])
        p.wait("scalar", wsem[sl], wv)
        p.wait("scalar", "pe", wbuf_free[sl])
        stage_free[sl] = p.op(
            "scalar", lambda e, s=stage[sl], w=wbuf[sl]: e.copy(out=w, in_=s),
            "ac")
        p.wait("tensor", "ac", stage_free[sl])
        p.wait("tensor", "ac", ps_free[sl])
        mmv = None
        for tt in range(4):
            for kc in range(KC):
                def mm(e, o=ps[sl][:, tt * 512:(tt + 1) * 512],
                       w=wbuf[sl][:, kc * 128:(kc + 1) * 128],
                       r=rhs_chunk(kc, slice(tt * 512, tt * 512 + 512)),
                       st=(kc == 0), sp=(kc == KC - 1)):
                    return e.matmul(o, w, r, start=st, stop=sp,
                                    skip_group_check=True)
                last = (tt == 3) and (kc == KC - 1)
                mmv = p.op("tensor", mm, "pe" if last else None)
        wbuf_free[sl] = mmv
        p.wait("scalar", "pe", mmv)
        p.wait("scalar", ssem[sl], slab_free[sl])
        if scale is None:
            ps_free[sl] = p.op(
                "scalar", lambda e, s=ps[sl][:], d=slabs[sl]: e.copy(
                    out=d, in_=s), "ac")
        else:
            ps_free[sl] = p.op(
                "scalar", lambda e, s=ps[sl][:], d=slabs[sl]: e.mul(
                    out=d, in_=s, mul=scale), "ac")
        p.wait("gpsimd", "ac", ps_free[sl])
        slab_free[sl] = p.dma("gpsimd", dest[oc], slabs[sl], ssem[sl])
    ph.close()


def ph_attention(p, nc, V, layer, heads, dest, sm):
    Abf, Bbf, Bf32 = V["Abf"], V["Bbf"], V["Bf32"]
    ph = ExitStack()
    psS = ph.enter_context(_pt(nc, "psS", [128, 2048], F32))
    psPV = ph.enter_context(_pt(nc, "psPV", [128, 512], F32))
    psZ = ph.enter_context(_pt(nc, "psZ", [1, 512], F32))
    psT = ph.enter_context(_pt(nc, "psT", [128, 256], BF16))
    psR = ph.enter_context(_pt(nc, "psR", [1, 512], F32))

    o = {}
    off = 0
    for nm, n in [("P1q", S), ("P2q", S), ("P1k", S), ("P2k", S),
                  ("Vt", S), ("QROT", S), ("KROT", S), ("SQq", S),
                  ("SQk", S), ("vT", S), ("pT", S),
                  ("outA", 512), ("outB", 512)]:
        o[nm] = Bbf[:, off:off + n]
        off += n
    cD = Abf[:, 0:2048]
    sD = Abf[:, 2048:4096]
    masks = Abf[:, 4096:6144]
    Ccol = Bf32[:, 12100:12101]
    sc1 = Bf32[0:1, 12102:12103]
    sc2 = Bf32[0:1, 12104:12105]
    kt = Bf32[0:1, 12106:12107]
    urow_f = Bf32[0:1, 12544:13056]
    uBf = Bf32[:, 13056:13568]
    cs_h = sm["cs_h"]
    urow_h = sm["urow_h"]
    ones_c, ones_r, negones_r = sm["ones_c"], sm["ones_r"], sm["negones_r"]
    ident = sm["ident"]

    out_free = [0, 0]
    n_out = 0
    prev_kv = -1
    for (h, qsrc, ksrc, vsrc, ochunk, kvid, causal) in heads:
        new_kv = (kvid != prev_kv)
        prev_kv = kvid
        p.barrier()
        p.dma("sync", o["P1q"], qsrc, "rA")
        p.dma("sync", o["P2q"][0:64, :], qsrc[64:128, :], "rA")
        lv = p.dma("sync", o["P2q"][64:128, :], qsrc[0:64, :], "rA")
        if new_kv:
            p.dma("sync", o["P1k"], ksrc, "rA")
            p.dma("sync", o["P2k"][0:64, :], ksrc[64:128, :], "rA")
            p.dma("sync", o["P2k"][64:128, :], ksrc[0:64, :], "rA")
            lv = p.dma("sync", o["Vt"], vsrc, "rA")
        p.wait("vector", "rA", lv)

        def rot(dst, p1, p2, tmp):
            p.op("vector", lambda e, dst=dst, p1=p1: e.tensor_mul(
                out=dst, in0=p1, in1=cD), "dv")
            mv = p.op("vector", lambda e, tmp=tmp, p2=p2: e.tensor_mul(
                out=tmp, in0=p2, in1=sD), "dv")
            p.wait("vector", "dv", mv)
            return p.op("vector", lambda e, dst=dst, tmp=tmp: e.tensor_add(
                out=dst, in0=dst, in1=tmp), "dv")

        rq = rot(o["QROT"], o["P1q"], o["P2q"], o["SQq"])
        if layer == 1:
            p.wait("vector", "dv", rq)
            rq = p.op("vector", lambda e: e.tensor_scalar(
                out=o["QROT"], in0=o["QROT"], scalar1=INVSQ, scalar2=None,
                op0=ALU.mult), "dv")
        if new_kv:
            rot(o["KROT"], o["P1k"], o["P2k"], o["SQk"])
        rot_done = p.cnt["dv"]

        p.wait("scalar", "dv", rot_done)
        p.op("scalar", lambda e: e.activation(out=o["SQq"], in_=o["QROT"],
                                              func=AF.Square), "ac")
        if new_kv:
            p.op("scalar", lambda e: e.activation(out=o["SQk"], in_=o["KROT"],
                                                  func=AF.Square), "ac")
        sq_done = p.cnt["ac"]
        p.wait("tensor", "ac", sq_done)

        def colmax(src, dst):
            mv = None
            for tt in range(4):
                mv = p.op("tensor", lambda e, tt=tt: e.matmul(
                    psS[0:1, tt * 512:(tt + 1) * 512], ones_c[:],
                    src[:, tt * 512:(tt + 1) * 512],
                    start=True, stop=True, skip_group_check=True),
                    "pe" if tt == 3 else None)
            p.wait("vector", "pe", mv)
            rv = p.op("vector", lambda e: e.reduce_max(
                out=dst, in_=psS[0:1, 0:2048], axis=AX.X), "dv")
            p.wait("tensor", "dv", rv)
            return rv

        colmax(o["SQq"], sc2)
        if new_kv:
            colmax(o["SQk"], sc1)
        mx_done = p.cnt["dv"]
        p.wait("vector", "dv", mx_done)
        cv = p.op("vector", lambda e: e.tensor_tensor(
            out=sc2, in0=sc2, in1=sc1, op=ALU.mult), "dv")
        p.wait("scalar", "dv", cv)
        cv = p.op("scalar", lambda e: e.activation(
            out=cs_h[:], in_=sc2, func=AF.Sqrt), "ac")
        p.wait("tensor", "ac", cv)
        cv = p.op("tensor", lambda e: e.matmul(
            psPV[:, 0:1], negones_r[:], cs_h[:], start=True, stop=True,
            skip_group_check=True), "pe")
        p.wait("vector", "pe", cv)
        cv = p.op("vector", lambda e: e.tensor_copy(
            out=Ccol, in_=psPV[:, 0:1]), "dv")
        p.wait("scalar", "dv", cv)
        ccol_done = cv

        if new_kv:
            tv = 0
            for tc in range(16):
                tsl = tc % 2
                p.wait("tensor", "ac", tv)
                mv = p.op("tensor", lambda e, tc=tc, tsl=tsl: e.transpose(
                    psT[:, tsl * 128:(tsl + 1) * 128],
                    o["Vt"][:, tc * 128:(tc + 1) * 128], ident[:]), "pe")
                p.wait("scalar", "pe", mv)
                tv = p.op("scalar", lambda e, tc=tc, tsl=tsl: e.copy(
                    out=o["vT"][:, tc * 128:(tc + 1) * 128],
                    in_=psT[:, tsl * 128:(tsl + 1) * 128]), "ac")
            p.wait("tensor", "ac", tv)

        psS_rel = 0
        pv_rel = 0
        pv_done = 0
        for st_ in range(4):
            ssl = slice(st_ * 512, (st_ + 1) * 512)
            ngr = (st_ + 1) if causal else 4
            for g in range(ngr):
                smm = None
                p.wait("tensor", "ac", psS_rel)
                for j in range(4):
                    tc = g * 4 + j
                    ssb = slice(j * 512, (j + 1) * 512)
                    smm = p.op("tensor", lambda e, tc=tc, ssb=ssb, ssl=ssl:
                               e.matmul(psS[:, ssb],
                                        o["KROT"][:, tc * 128:(tc + 1) * 128],
                                        o["QROT"][:, ssl], start=True,
                                        stop=True, skip_group_check=True),
                               "pe" if j == 3 else None)
                p.wait("scalar", "pe", smm)
                ev = p.op("scalar", lambda e: e.activation(
                    out=o["pT"], in_=psS[:], func=AF.Exp,
                    bias=Ccol, scale=1.0), "ac")
                psS_rel = ev
                dep = ("ac", ev)
                if causal and g == st_:
                    p.wait("vector", "ac", ev)
                    mv = None
                    for j in range(4):
                        ssb = slice(j * 512, (j + 1) * 512)
                        mv = p.op("vector", lambda e, ssb=ssb, mi=j:
                                  e.tensor_tensor(
                                      out=o["pT"][:, ssb],
                                      in0=o["pT"][:, ssb],
                                      in1=masks[:, mi * 512:(mi + 1) * 512],
                                      op=ALU.mult), "dv")
                    dep = ("dv", mv)
                p.wait("tensor", dep[0], dep[1])
                if g == 0:
                    p.wait("tensor", "dv", pv_rel)
                for j in range(4):
                    tc = g * 4 + j
                    ssb = slice(j * 512, (j + 1) * 512)
                    first = (g == 0 and j == 0)
                    lastc = (g == ngr - 1 and j == 3)
                    p.op("tensor", lambda e, ssb=ssb, first=first,
                         lastc=lastc:
                         e.matmul(psZ[:], ones_c[:], o["pT"][:, ssb],
                                  start=first, stop=lastc,
                                  skip_group_check=True), None)
                    pv_done = p.op(
                        "tensor", lambda e, ssb=ssb, tc=tc, first=first,
                        lastc=lastc: e.matmul(
                            psPV[:], o["vT"][:, tc * 128:(tc + 1) * 128],
                            o["pT"][:, ssb], start=first, stop=lastc,
                            skip_group_check=True),
                        "pe" if lastc else None)
            p.wait("vector", "pe", pv_done)
            v5 = p.op("vector", lambda e: e.reciprocal(out=urow_f,
                                                       in_=psZ[:]), "dv")
            p.wait("vector", "dv", v5)
            v6 = p.op("vector", lambda e: e.tensor_copy(out=urow_h[:],
                                                        in_=urow_f), "dv")
            p.wait("tensor", "dv", v6)
            p.wait("tensor", "ac", psS_rel)
            ub = p.op("tensor", lambda e: e.matmul(
                psS[:, 0:512], ones_r[:], urow_h[:], start=True, stop=True,
                skip_group_check=True), "pe")
            p.wait("scalar", "pe", ub)
            p.wait("scalar", "dv", pv_rel)
            ub = p.op("scalar", lambda e: e.copy(out=uBf,
                                                 in_=psS[:, 0:512]), "ac")
            psS_rel = ub
            osl = n_out % 2
            n_out += 1
            osem = "sA" if osl == 0 else "sB"
            obuf = o["outA"] if osl == 0 else o["outB"]
            p.wait("vector", "ac", ub)
            p.wait("vector", osem, out_free[osl])
            v8 = p.op("vector", lambda e, obuf=obuf: e.tensor_tensor(
                out=obuf, in0=psPV[:], in1=uBf, op=ALU.mult), "dv")
            pv_rel = v8
            p.wait("gpsimd", "dv", v8)
            out_free[osl] = p.dma(
                "gpsimd", dest[ochunk][:, ssl], obuf, osem)
    ph.close()


def ph_resid_ln(p, nc, V, w_hbm, KC, rhs_chunk, xold, out, out_f32, sm,
                bias_cols=None, wbf16=False, resid_scr=None):
    Abf, Bbf, Bf32 = V["Abf"], V["Bbf"], V["Bf32"]
    OC = DC
    HF = KC * 64
    ph = ExitStack()
    psW = [ph.enter_context(_pt(nc, "psW", [128, 1024], F32))
           for _ in range(2)]
    psSum = ph.enter_context(_pt(nc, "psSum", [1, 1024], F32))
    psSq = ph.enter_context(_pt(nc, "psSq", [1, 1024], F32))
    stage = [Bf32[:, 0:HF], Bf32[:, 2048:2048 + HF]]
    wbuf = [Bbf[:, 8192:8192 + KC * 128], Bbf[:, 12288:12288 + KC * 128]]
    rslab = Bbf[:, 16384:17408]
    xslab = Bbf[:, 17408:18432]
    sqslab = Bbf[:, 18432:19456]
    meanB = Bbf[:, 19456:20480]
    rstdB = Bbf[:, 20480:21504]
    nslab = Bbf[:, 21504:22528]
    oslab_f = Bf32[:, 11264:12288]
    lnrows = sm["lnrows"]
    lnrow_h = sm["lnrow_h"]
    ones_c, ones_r = sm["ones_c"], sm["ones_r"]

    stg_free = [0, 0]
    wbuf_free = [0, 0]
    psW_free = [0, 0]
    wsem = ["wA", "wB"]
    for th in range(2):
        tw = slice(th * 1024, (th + 1) * 1024)
        rv = 0
        sv = 0
        st_pe = 0
        xv = 0
        sqv = 0
        for oc in range(OC):
            sl = oc % 2
            if not wbf16:
                for hf in range(2):
                    p.wait("sync", "ac", stg_free[hf])
                    wv = p.dma("sync", stage[hf],
                               w_hbm[oc][:, hf * HF:(hf + 1) * HF], wsem[hf])
                    p.wait("scalar", wsem[hf], wv)
                    if hf == 0:
                        p.wait("scalar", "pe", wbuf_free[sl])
                    stg_free[hf] = p.op(
                        "scalar", lambda e, s=stage[hf],
                        w=wbuf[sl][:, hf * HF:(hf + 1) * HF]:
                        e.copy(out=w, in_=s), "ac")
                cast_done = stg_free[1]
                p.wait("tensor", "ac", cast_done)
            else:
                p.wait("sync", "pe", wbuf_free[sl])
                wv = p.dma("sync", wbuf[sl], w_hbm[oc], wsem[sl])
                p.wait("tensor", wsem[sl], wv)
            if xold is not None:
                p.wait("gpsimd", "dv", xv)
                xv16 = p.dma("gpsimd", xslab, xold[oc][:, tw], "gA")
            p.wait("tensor", "dv", psW_free[sl])
            mmv = None
            for tt in range(2):
                for kc in range(KC):
                    def mm(e, o=psW[sl][:, tt * 512:(tt + 1) * 512],
                           w=wbuf[sl][:, kc * 128:(kc + 1) * 128],
                           r=rhs_chunk(kc, slice(th * 1024 + tt * 512,
                                                 th * 1024 + tt * 512 + 512)),
                           st=(kc == 0), sp=(kc == KC - 1)):
                        return e.matmul(o, w, r, start=st, stop=sp,
                                        skip_group_check=True)
                    last = (tt == 1) and (kc == KC - 1)
                    mmv = p.op("tensor", mm, "pe" if last else None)
            wbuf_free[sl] = mmv
            p.wait("vector", "pe", mmv)
            p.wait("vector", "pe", st_pe)
            p.wait("vector", "ac", sqv)
            p.wait("vector", "sA", sv)
            if xold is not None:
                p.wait("vector", "gA", xv16)
                xv = None
                xin = xslab
            else:
                xin = rhs_chunk(oc, tw) if KC == DC else None
                assert xin is not None
            if bias_cols is not None:
                rv = p.op("vector", lambda e, sps=psW[sl], xin=xin, oc=oc,
                          d=rslab: e.scalar_tensor_tensor(
                              out=d, in0=sps[:],
                              scalar=bias_cols[:, oc:oc + 1],
                              in1=xin, op0=ALU.add, op1=ALU.add), "dv")
            else:
                rv = p.op("vector", lambda e, sps=psW[sl], xin=xin,
                          d=rslab: e.tensor_add(out=d, in0=sps[:],
                                                in1=xin), "dv")
            psW_free[sl] = rv
            if xold is not None:
                xv = rv
            p.wait("gpsimd", "dv", rv)
            sv = p.dma("gpsimd", resid_scr[oc][:, tw], rslab, "sA")
            p.wait("scalar", "dv", rv)
            p.wait("scalar", "pe", st_pe)
            sqv = p.op("scalar", lambda e, s=rslab, d=sqslab:
                       e.activation(out=d, in_=s, func=AF.Square), "ac")
            p.wait("tensor", "dv", rv)
            p.wait("tensor", "ac", sqv)
            for half in range(2):
                hs = slice(half * 512, (half + 1) * 512)
                p.op("tensor", lambda e, hs=hs, s=rslab, oc=oc:
                     e.matmul(psSum[:, hs], ones_c[:], s[:, hs],
                              start=(oc == 0), stop=(oc == OC - 1),
                              skip_group_check=True), None)
                st_pe = p.op(
                    "tensor", lambda e, hs=hs, s=sqslab, oc=oc:
                    e.matmul(psSq[:, hs], ones_c[:], s[:, hs],
                             start=(oc == 0), stop=(oc == OC - 1),
                             skip_group_check=True),
                    "pe" if half == 1 else None)
        p.wait("vector", "pe", st_pe)
        p.wait("scalar", "pe", st_pe)
        mh = p.op("vector", lambda e, th=th: e.tensor_scalar(
            out=lnrow_h[0:1, th * 2048:th * 2048 + 1024], in0=psSum[:],
            scalar1=1.0 / D, scalar2=None, op0=ALU.mult), "dv")
        v1 = p.op("vector", lambda e: e.tensor_scalar(
            out=lnrows[0:1, 0:1024], in0=psSq[:], scalar1=1.0 / D,
            scalar2=None, op0=ALU.mult), "dv")
        p.wait("scalar", "dv", mh)
        v2 = p.op("scalar", lambda e: e.activation(
            out=psSum[:], in_=psSum[:], func=AF.Square, scale=1.0 / D), "ac")
        p.wait("vector", "ac", v2)
        p.wait("vector", "dv", v1)
        v3 = p.op("vector", lambda e: e.scalar_tensor_tensor(
            out=lnrows[0:1, 0:1024], in0=lnrows[0:1, 0:1024], scalar=EPS,
            in1=psSum[:], op0=ALU.add, op1=ALU.subtract), "dv")
        p.wait("scalar", "dv", v3)
        v4 = p.op("scalar", lambda e: e.activation(
            out=psSum[:], in_=lnrows[0:1, 0:1024], func=AF.Sqrt), "ac")
        p.wait("vector", "ac", v4)
        v5 = p.op("vector", lambda e: e.reciprocal(
            out=lnrows[0:1, 0:1024], in_=psSum[:]), "dv")
        p.wait("vector", "dv", v5)
        p.op("vector", lambda e, th=th: e.tensor_copy(
            out=lnrow_h[0:1, th * 2048 + 1024:th * 2048 + 2048],
            in_=lnrows[0:1, 0:1024]), "dv")
        p.barrier()
    for th in range(2):
        tw = slice(th * 1024, (th + 1) * 1024)
        for which, dstB in [(0, meanB), (1, rstdB)]:
            base = th * 2048 + which * 1024
            p.op("tensor", lambda e, base=base: e.matmul(
                psW[0][:, 0:512], ones_r[:],
                lnrow_h[0:1, base:base + 512], start=True, stop=True,
                skip_group_check=True), None)
            bv = p.op("tensor", lambda e, base=base: e.matmul(
                psW[0][:, 512:1024], ones_r[:],
                lnrow_h[0:1, base + 512:base + 1024], start=True, stop=True,
                skip_group_check=True), "pe")
            p.wait("scalar", "pe", bv)
            bv = p.op("scalar", lambda e, dstB=dstB: e.copy(
                out=dstB, in_=psW[0][:]), "ac")
            p.wait("tensor", "ac", bv)
        p.wait("vector", "ac", p.cnt["ac"])
        nv_rel = 0
        st_rel = 0
        for oc in range(DC):
            p.wait("gpsimd", "dv", nv_rel)
            p.wait("gpsimd", "sB", st_rel)
            nv = p.dma("gpsimd", nslab, resid_scr[oc][:, tw], "gC")
            p.wait("vector", "gC", nv)
            p.wait("vector", "sB", st_rel)
            osl = oslab_f if out_f32 else nslab
            vv = p.op("vector", lambda e, osl=osl: e.tensor_tensor(
                out=osl, in0=nslab, in1=meanB, op=ALU.subtract), "dv")
            p.wait("vector", "dv", vv)
            vv = p.op("vector", lambda e, osl=osl: e.tensor_tensor(
                out=osl, in0=osl, in1=rstdB, op=ALU.mult), "dv")
            nv_rel = vv
            p.wait("gpsimd", "dv", vv)
            if out_f32:
                st_rel = p.dma("gpsimd", out[oc * 128:(oc + 1) * 128, tw],
                               oslab_f, "sB")
            else:
                st_rel = p.dma("gpsimd", out[oc][:, tw], nslab, "sB")
        p.barrier()
    ph.close()


def ph_swiglu(p, nc, V, v1_b, b1s, p_chunks, act_scr):
    Abf, Bbf, Bf32 = V["Abf"], V["Bbf"], V["Bf32"]
    ph = ExitStack()
    psG = ph.enter_context(_pt(nc, "psG", [128, 2048], F32))
    psU = ph.enter_context(_pt(nc, "psU", [128, 2048], F32))
    PB = Abf[:, 0:16384]
    stage = [Bf32[:, 0:1024], Bf32[:, 1024:2048]]
    wbuf = [Bbf[:, 4096:5120], Bbf[:, 5120:6144]]
    silu_t = [Bbf[:, 6144:8192], Bbf[:, 8192:10240]]
    actsl = [Bbf[:, 10240:12288], Bbf[:, 12288:14336]]
    for kc in range(8):
        p.dma("sync", PB[:, kc * 2048:(kc + 1) * 2048], p_chunks[kc], "rA")
    p.barrier()
    wsem = ["wA", "wB"]
    ssem = ["sA", "sB"]
    stage_free = [0, 0]
    wbuf_free = [0, 0]
    silu_free = [0, 0]
    act_store = [0, 0]
    gate_rel = 0
    up_rel = 0
    for i in range(86):
        done = {}
        for gu in range(2):
            oc = i if gu == 0 else 86 + i
            sl = gu
            p.wait("sync", "ac", stage_free[sl])
            wv = p.dma("sync", stage[sl], v1_b[oc], wsem[sl])
            p.wait("scalar", wsem[sl], wv)
            p.wait("scalar", "pe", wbuf_free[sl])
            stage_free[sl] = p.op(
                "scalar", lambda e, s=stage[sl], w=wbuf[sl]: e.copy(
                    out=w, in_=s), "ac")
            p.wait("tensor", "ac", stage_free[sl])
            if gu == 0:
                p.wait("tensor", "ac", gate_rel)
            else:
                p.wait("tensor", "dv", up_rel)
            tgt = psG if gu == 0 else psU
            mmv = None
            for tt in range(4):
                for kc in range(8):
                    def mm(e, o=tgt[:, tt * 512:(tt + 1) * 512],
                           w=wbuf[sl][:, kc * 128:(kc + 1) * 128],
                           r=PB[:, kc * 2048 + tt * 512:
                                kc * 2048 + tt * 512 + 512],
                           st=(kc == 0), sp=(kc == 7)):
                        return e.matmul(o, w, r, start=st, stop=sp,
                                        skip_group_check=True)
                    last = (tt == 3) and (kc == 7)
                    mmv = p.op("tensor", mm, "pe" if last else None)
            done[gu] = mmv
            wbuf_free[sl] = mmv
        ssl = i % 2
        p.wait("scalar", "pe", done[0])
        p.wait("scalar", "dv", silu_free[ssl])
        gate_rel = p.op("scalar", lambda e, i=i, ssl=ssl: e.activation(
            out=silu_t[ssl], in_=psG[:], func=AF.Silu,
            bias=b1s[:, i:i + 1]), "ac")
        p.wait("vector", "pe", done[1])
        p.wait("vector", "ac", gate_rel)
        p.wait("vector", ssem[ssl], act_store[ssl])
        up_rel = p.op("vector", lambda e, i=i, ssl=ssl: e.scalar_tensor_tensor(
            out=actsl[ssl], in0=psU[:], scalar=b1s[:, 86 + i:86 + i + 1],
            in1=silu_t[ssl], op0=ALU.add, op1=ALU.mult), "dv")
        silu_free[ssl] = up_rel
        p.wait("gpsimd", "dv", up_rel)
        for tt in range(4):
            act_store[ssl] = p.dma("gpsimd", act_scr[i][tt],
                                   actsl[ssl][:, tt * 512:(tt + 1) * 512],
                                   ssem[ssl])
    ph.close()


def ph_z(p, nc, V, u2_b, act_scr, z_scr):
    Abf, Bbf, Bf32 = V["Abf"], V["Bbf"], V["Bf32"]
    ph = ExitStack()
    psZ2 = [ph.enter_context(_pt(nc, "psZ2", [128, 2048], F32))
            for _ in range(2)]
    U2H = Abf[:, 0:86 * 512]
    stage = [Bf32[:, 0:512], Bf32[:, 512:1024]]
    aslab = [Bbf[:, 2048 + i * 512:2048 + (i + 1) * 512] for i in range(4)]
    zslab = [Bbf[:, 4096:4608], Bbf[:, 4608:5120]]
    wsem = ["wA", "wB"]
    a_pe = [0, 0, 0, 0]
    z_store = [0, 0]
    ps_rel = [0, 0]
    stage_free = [0, 0]
    for rh in range(2):
        if rh == 1:
            p.wait("scalar", "pe", p.cnt["pe"])
        for kc in range(86):
            sl = kc % 2
            p.wait("sync", "ac", stage_free[sl])
            wv = p.dma("sync", stage[sl], u2_b[rh, kc], wsem[sl])
            p.wait("scalar", wsem[sl], wv)
            stage_free[sl] = p.op(
                "scalar", lambda e, s=stage[sl], kc=kc: e.copy(
                    out=U2H[:, kc * 512:(kc + 1) * 512], in_=s), "ac")
        cast_all = p.cnt["ac"]
        p.wait("tensor", "ac", cast_all)
        for tt in range(4):
            psl = tt % 2
            p.wait("tensor", "dv", ps_rel[psl])
            for kc in range(86):
                asl = kc % 4
                qeng = "sync" if (kc % 2 == 0) else "gpsimd"
                qsem = ("qA", "gA", "qB", "gB")[asl]
                p.wait(qeng, "pe", a_pe[asl])
                av = p.dma(qeng, aslab[asl], act_scr[kc][tt], qsem)
                p.wait("tensor", qsem, av)
                for rc in range(4):
                    def mm(e, o=psZ2[psl][:, rc * 512:(rc + 1) * 512],
                           w=U2H[:, kc * 512 + rc * 128:
                                 kc * 512 + rc * 128 + 128],
                           r=aslab[asl], st=(kc == 0), sp=(kc == 85)):
                        return e.matmul(o, w, r, start=st, stop=sp,
                                        skip_group_check=True)
                    a_pe[asl] = p.op("tensor", mm, "pe" if rc == 3 else None)
            mm_done = p.cnt["pe"]
            p.wait("vector", "pe", mm_done)
            for rc in range(4):
                zsl = rc % 2
                p.wait("vector", ("sA", "sB")[zsl], z_store[zsl])
                zv = p.op("vector", lambda e, rc=rc, psl=psl, zsl=zsl:
                          e.tensor_copy(
                              out=zslab[zsl],
                              in_=psZ2[psl][:, rc * 512:(rc + 1) * 512]),
                          "dv")
                p.wait("gpsimd", "dv", zv)
                z_store[zsl] = p.dma(
                    "gpsimd",
                    z_scr[rh * 4 + rc][:, tt * 512:(tt + 1) * 512],
                    zslab[zsl], ("sA", "sB")[zsl])
            ps_rel[psl] = p.cnt["dv"]
    ph.close()


def ph_dequant(p, nc, V, codesT, codes_bf, wb_dram, sm):
    Bbf, Bf32, Bi32 = V["Bbf"], V["Bf32"], V["Bi32"]
    ph = ExitStack()
    psBC = [ph.enter_context(_pt(nc, "psBC", [128, 512], F32))
            for _ in range(2)]
    psWd = ph.enter_context(_pt(nc, "psWd", [8, 512], F32))
    psSc = ph.enter_context(_pt(nc, "psSc", [8, 512], F32))
    ci32 = Bi32[:, 0:4096]
    cf32 = Bf32[:, 4096:8192]
    cbf = Bbf[:, 16384:20480]
    strip = Bbf[0:1, 1024:9216]
    oh = Bbf[:, 20480:22528]
    wline = Bbf[0:8, 22528:26624]
    iot, e01, cbf32 = sm["iot"], sm["e01"], sm["cbf32"]
    scalef, scaleB = sm["scalef"], sm["scaleB"]
    ones_r, ones8f = sm["ones_r"], sm["ones8f"]

    v = p.op("vector", lambda e: e.tensor_copy(out=e01[:], in_=cbf32[:]), "dv")
    sv = p.op("tensor", lambda e: e.matmul(
        psSc[:], ones8f[:], scalef[:], start=True, stop=True,
        skip_group_check=True), "pe")
    p.wait("vector", "pe", sv)
    sv = p.op("vector", lambda e: e.tensor_copy(out=scaleB[:],
                                                in_=psSc[:]), "dv")
    cl = p.dma("sync", ci32,
               codesT.rearrange("j c o -> (j c o)").rearrange(
                   "(q n) -> q n", q=128), "rA")
    p.wait("vector", "rA", cl)
    v1 = p.op("vector", lambda e: e.tensor_copy(out=cf32, in_=ci32), "dv")
    p.wait("vector", "dv", v1)
    v2 = p.op("vector", lambda e: e.tensor_copy(out=cbf, in_=cf32), "dv")
    p.wait("gpsimd", "dv", v2)
    v3 = p.dma("gpsimd",
               codes_bf.rearrange("j c o -> (j c o)").rearrange(
                   "(q n) -> q n", q=128), cbf, "sA")
    p.wait("sync", "sA", v3)
    p.barrier()

    strip_pe = 0
    oh_dv = 0
    wl_st = 0
    for jg in range(64):
        p.wait("sync", "pe", strip_pe)
        stv = p.dma("sync", strip,
                    codes_bf[jg * 8:(jg + 1) * 8].rearrange(
                        "j c o -> (j c o)")[None, :], "qA")
        p.wait("tensor", "qA", stv)
        for jj in range(8):
            for cb in range(2):
                p.wait("tensor", "dv", oh_dv)
                strip_pe = p.op(
                    "tensor", lambda e, jj=jj, cb=cb: e.matmul(
                        psBC[cb][:], ones_r[:],
                        strip[0:1, (jj * 2 + cb) * 512:
                              (jj * 2 + cb) * 512 + 512],
                        start=True, stop=True, skip_group_check=True), "pe")
            p.wait("vector", "pe", strip_pe)
            for cc in range(4):
                oh_dv = p.op(
                    "vector", lambda e, cc=cc:
                    e.tensor_scalar(out=oh[:, cc * 512:(cc + 1) * 512],
                                    in0=psBC[cc // 2][:],
                                    scalar1=iot[:, (cc % 2):(cc % 2) + 1],
                                    scalar2=None, op0=ALU.is_equal), "dv")
            p.wait("tensor", "dv", oh_dv)
            wdv = None
            for cc in range(4):
                wdv = p.op("tensor", lambda e, cc=cc: e.matmul(
                    psWd[:], e01[:, cc * 8:(cc + 1) * 8],
                    oh[:, cc * 512:(cc + 1) * 512],
                    start=(cc == 0), stop=(cc == 3), skip_group_check=True),
                    "pe" if cc == 3 else None)
            p.wait("vector", "pe", wdv)
            p.wait("vector", "gC", wl_st)
            wlv = p.op("vector", lambda e, jj=jj: e.tensor_tensor(
                out=wline[:, jj * 512:(jj + 1) * 512],
                in0=psWd[:], in1=scaleB[:], op=ALU.mult), "dv")
            p.wait("tensor", "dv", wlv)
        wl_done = p.cnt["dv"]
        p.wait("gpsimd", "dv", wl_done)
        wl_st = p.dma(
            "gpsimd",
            wb_dram[jg * 64:(jg + 1) * 64, :].rearrange(
                "(g j) o -> j g o", j=8),
            wline.rearrange("j (g o) -> j g o", g=8), "gC")
    ph.close()


def _smalls(nc, es, p, names):
    sm = {}
    mk = {
        "ident": ([128, 128], BF16), "ones_c": ([128, 1], BF16),
        "ones_r": ([1, 128], BF16), "negones_r": ([1, 128], BF16),
        "ones8f": ([1, 8], F32),
        "lnrows": ([1, 1024], F32), "lnrow_h": ([1, 4096], BF16),
        "cs_h": ([1, 1], BF16), "urow_h": ([1, 512], BF16),
        "iot": ([128, 2], F32), "e01": ([128, 32], BF16),
        "cbf32": ([128, 32], F32), "scalef": ([1, 512], F32),
        "scaleB": ([8, 512], F32), "bias32": ([128, 32], F32),
        "b1s": ([128, 172], F32),
    }
    for n in names:
        sm[n] = es.enter_context(
            nc.sbuf_tensor(f"{n}_{next(_uid)}", *mk[n]))[:]
    memsets = {"ones_c": 1.0, "ones_r": 1.0, "negones_r": -1.0, "ones8f": 1.0}
    mv = None
    for n in names:
        if n in memsets:
            mv = p.op("vector", lambda e, t=sm[n], v=memsets[n]:
                      e.memset(t[:], v), "dv")
    if mv is not None:
        p.barrier()
    return sm


def _din(nc, name, shape, dt=F32):
    return nc.dram_tensor(name, shape, dt, kind="ExternalInput").ap()


def _dout(nc, name, shape, dt=BF16):
    return nc.dram_tensor(name, shape, dt, kind="ExternalOutput").ap()


def build_qkv0():
    nc, es, p, V = _mk(131072, 32768)
    x_fm = _din(nc, "x_fm", [D, S])
    wq_b = _din(nc, "wq_b", [DC, 128, D])
    wk_b = _din(nc, "wk_b", [DC, 128, D])
    wv_b = _din(nc, "wv_b", [DC, 128, D])
    xbf = _dout(nc, "xbf", [DC, 128, S])
    q_scr = _dout(nc, "q_scr", [NH, 128, S])
    k_scr = _dout(nc, "k_scr", [NH, 128, S])
    v_scr = _dout(nc, "v_scr", [NH, 128, S])
    Abf = V["Abf"]

    def xch(kc, ts):
        return Abf[:, kc * S:(kc + 1) * S][:, ts]

    ph_load_x(p, V, x_fm, xbf)
    p.barrier()
    ph_proj(p, nc, V, wq_b, DC, NH, xch, q_scr, scale=INVSQ)
    p.barrier()
    ph_proj(p, nc, V, wk_b, DC, NH, xch, k_scr)
    p.barrier()
    ph_proj(p, nc, V, wv_b, DC, NH, xch, v_scr)
    p.barrier()
    p.emit()
    return nc, es


def build_attn(layer):
    nc, es, p, V = _mk(16384, 54272)
    sm = _smalls(nc, es, p, ["ident", "ones_c", "ones_r", "negones_r",
                             "cs_h", "urow_h"])
    cD_in = _din(nc, "cD_in", [128, S], BF16)
    sD_in = _din(nc, "sD_in", [128, S], BF16)
    ident_in = _din(nc, "ident_in", [128, 128], BF16)
    attn_scr = _dout(nc, "attn_scr", [DC, 128, S])
    Abf = V["Abf"]
    p.dma("sync", Abf[:, 0:2048], cD_in, "rA")
    p.dma("sync", Abf[:, 2048:4096], sD_in, "rA")
    p.dma("sync", sm["ident"][:], ident_in, "rA")
    if layer == 0:
        q_scr = _din(nc, "q_scr", [NH, 128, S], BF16)
        k_scr = _din(nc, "k_scr", [NH, 128, S], BF16)
        v_scr = _din(nc, "v_scr", [NH, 128, S], BF16)
        heads = [(h, q_scr[h], k_scr[h], v_scr[h], h, h, False)
                 for h in range(NH)]
    else:
        masks_in = _din(nc, "masks_in", [128, S], BF16)
        p.dma("sync", Abf[:, 4096:6144], masks_in, "rA")
        qkv_scr = _din(nc, "qkv_scr", [48, 128, S], BF16)
        heads = [(4 * hk + g, qkv_scr[4 * hk + g], qkv_scr[32 + hk],
                  qkv_scr[40 + hk], 4 * hk + g, hk, True)
                 for hk in range(8) for g in range(4)]
    p.barrier()
    ph_attention(p, nc, V, layer, heads, attn_scr, sm)
    p.barrier()
    p.emit()
    return nc, es


def build_resid_ln(name_w, KC, rhs_name, rhs_nch, xold_name, out_f32,
                   bias=False, wbf16=False):
    nc, es, p, V = _mk(131072, 45056)
    names = ["ones_c", "ones_r", "lnrows", "lnrow_h"]
    if bias:
        names.append("bias32")
    sm = _smalls(nc, es, p, names)
    w_hbm = _din(nc, name_w, [DC, 128, KC * 128], BF16 if wbf16 else F32)
    rhs = _din(nc, rhs_name, [rhs_nch, 128, S], BF16)
    xold = _din(nc, xold_name, [DC, 128, S], BF16) if xold_name else None
    bias_t = _din(nc, "bias_t", [128, DC]) if bias else None
    resid_scr = nc.dram_tensor(f"resid_{next(_uid)}", [DC, 128, S],
                               BF16).ap()
    if out_f32:
        out = nc.dram_tensor("y", [D, S], F32, kind="ExternalOutput").ap()
    else:
        out = _dout(nc, "xout", [DC, 128, S])
    Abf = V["Abf"]
    ph_load_A(p, Abf, [rhs[c] for c in range(rhs_nch)])
    if bias:
        p.dma("sync", sm["bias32"][:], bias_t, "rA")
    p.barrier()

    def rhs_chunk(kc, ts):
        return Abf[:, kc * S:(kc + 1) * S][:, ts]

    ph_resid_ln(p, nc, V, w_hbm, KC, rhs_chunk, xold, out, out_f32, sm,
                bias_cols=sm["bias32"] if bias else None, wbf16=wbf16,
                resid_scr=resid_scr)
    p.barrier()
    p.emit()
    return nc, es


def build_p0():
    nc, es, p, V = _mk(131072, 32768)
    x1 = _din(nc, "x1", [DC, 128, S], BF16)
    u1_b = _din(nc, "u1_b", [8, 128, D])
    p_scr = _dout(nc, "p_scr", [8, 128, S])
    Abf = V["Abf"]
    ph_load_A(p, Abf, [x1[c] for c in range(DC)])
    p.barrier()

    def xch(kc, ts):
        return Abf[:, kc * S:(kc + 1) * S][:, ts]

    ph_proj(p, nc, V, u1_b, DC, 8, xch, p_scr)
    p.barrier()
    p.emit()
    return nc, es


def build_swiglu():
    nc, es, p, V = _mk(32768, 16384)
    sm = _smalls(nc, es, p, ["b1s"])
    p_scr = _din(nc, "p_scr", [8, 128, S], BF16)
    v1_b = _din(nc, "v1_b", [172, 128, 1024])
    b1T = _din(nc, "b1T", [128, 172])
    act_scr = _dout(nc, "act_scr", [86, 4, 128, 512])
    p.dma("sync", sm["b1s"][:], b1T, "rA")
    p.barrier()
    ph_swiglu(p, nc, V, v1_b, sm["b1s"], [p_scr[i] for i in range(8)],
              act_scr)
    p.barrier()
    p.emit()
    return nc, es


def build_z():
    nc, es, p, V = _mk(90112, 8192)
    act_scr = _din(nc, "act_scr", [86, 4, 128, 512], BF16)
    u2_b = _din(nc, "u2_b", [2, 86, 128, 512])
    z_scr = _dout(nc, "z_scr", [8, 128, S])
    ph_z(p, nc, V, u2_b, act_scr, z_scr)
    p.barrier()
    p.emit()
    return nc, es


def build_qkv1():
    nc, es, p, V = _mk(131072, 32768)
    x2 = _din(nc, "x2", [DC, 128, S], BF16)
    win_b = _din(nc, "win_b", [48, 128, D])
    qkv_scr = _dout(nc, "qkv_scr", [48, 128, S])
    Abf = V["Abf"]
    ph_load_A(p, Abf, [x2[c] for c in range(DC)])
    p.barrier()

    def xch(kc, ts):
        return Abf[:, kc * S:(kc + 1) * S][:, ts]

    ph_proj(p, nc, V, win_b, DC, 48, xch, qkv_scr)
    p.barrier()
    p.emit()
    return nc, es


def build_dequant():
    nc, es, p, V = _mk(0, 61440)
    sm = _smalls(nc, es, p, ["iot", "e01", "cbf32", "scalef", "scaleB",
                             "ones_r", "ones8f"])
    codesT = _din(nc, "codesT", [512, 2, 512], I32)
    cbE = _din(nc, "cbE", [128, 32])
    myscale = _din(nc, "myscale", [1, 512])
    iot_in = _din(nc, "iot_in", [128, 2])
    wb_dram = _dout(nc, "wb_dram", [4096, 512])
    codes_bf = nc.dram_tensor("codes_bf", [512, 2, 512], BF16).ap()
    p.dma("sync", sm["cbf32"][:], cbE, "rA")
    p.dma("sync", sm["scalef"][:], myscale, "rA")
    p.dma("sync", sm["iot"][:], iot_in, "rA")
    p.barrier()
    ph_dequant(p, nc, V, codesT, codes_bf, wb_dram, sm)
    p.barrier()
    p.emit()
    return nc, es


def build_fused(mock_cc=False):
    nc, es, p, V = _mk(131072, 61440)
    Abf, Bbf, Bf32 = V["Abf"], V["Bbf"], V["Bf32"]
    sm = _smalls(nc, es, p, ["ident", "ones_c", "ones_r", "negones_r",
                             "ones8f", "cs_h", "urow_h", "iot", "e01",
                             "cbf32", "bias32", "b1s"])
    smR = dict(sm)
    smR["lnrows"] = Bf32[0:1, 12288:13312]
    smR["lnrow_h"] = Bbf[0:1, 26624:30720]
    smD = dict(sm)
    smD["scalef"] = Bf32[0:1, 13312:13824]
    smD["scaleB"] = Bf32[0:8, 13824:14336]

    x_fm = _din(nc, "x_fm", [D, S])
    wq_b = _din(nc, "wq_b", [DC, 128, D])
    wk_b = _din(nc, "wk_b", [DC, 128, D])
    wv_b = _din(nc, "wv_b", [DC, 128, D])
    wo_b = _din(nc, "wo_b", [DC, 128, D])
    u1_b = _din(nc, "u1_b", [8, 128, D])
    v1_b = _din(nc, "v1_b", [172, 128, 1024])
    u2_b = _din(nc, "u2_b", [2, 86, 128, 512])
    v2_b = _din(nc, "v2_b", [DC, 128, 1024])
    win_b = _din(nc, "win_b", [48, 128, D])
    wout_b = _din(nc, "wout_b", [DC, 128, D])
    b1T = _din(nc, "b1T", [128, 172])
    b2T = _din(nc, "b2T", [128, DC])
    bL1T = _din(nc, "bL1T", [128, DC])
    codesT = _din(nc, "codesT", [512, 2, 512], I32)
    cbE = _din(nc, "cbE", [128, 32])
    myscale = _din(nc, "myscale", [1, 512])
    iot_in = _din(nc, "iot_in", [128, 2])
    cD_in = _din(nc, "cD_in", [128, S], BF16)
    sD_in = _din(nc, "sD_in", [128, S], BF16)
    ident_in = _din(nc, "ident_in", [128, 128], BF16)
    masks_in = _din(nc, "masks_in", [128, S], BF16)
    y = nc.dram_tensor("y", [D, S], F32, kind="ExternalOutput").ap()

    def scr(name, shape, dt=BF16):
        return nc.dram_tensor(name, shape, dt).ap()

    xbf = scr("xbf", [DC, 128, S])
    q_scr = scr("q_scr", [NH, 128, S])
    k_scr = scr("k_scr", [NH, 128, S])
    v_scr = scr("v_scr", [NH, 128, S])
    attn_scr = scr("attn_scr", [DC, 128, S])
    x1 = scr("x1", [DC, 128, S])
    p_scr = scr("p_scr", [8, 128, S])
    act_scr = scr("act_scr", [86, 4, 128, 512])
    z_scr = scr("z_scr", [8, 128, S])
    x2 = scr("x2", [DC, 128, S])
    qkv_scr = scr("qkv_scr", [48, 128, S])
    x3 = scr("x3", [DC, 128, S])
    resid_scr = scr("resid_scr", [DC, 128, S])
    codes_bf = scr("codes_bf", [512, 2, 512])
    wb_dram = scr("wb_dram", [4096, 512])
    wfull = scr("wfull", [8 * 4096, 512])
    wfin4 = scr("wfin4", [DC, 128, 32, 128])

    def xch(kc, ts):
        return Abf[:, kc * S:(kc + 1) * S][:, ts]

    p.dma("sync", sm["ident"][:], ident_in, "rA")
    p.dma("sync", sm["iot"][:], iot_in, "rA")
    p.dma("sync", sm["cbf32"][:], cbE, "rA")
    p.dma("sync", sm["b1s"][:], b1T, "rA")
    p.dma("sync", smD["scalef"][:], myscale, "rA")
    p.barrier()

    ph_dequant(p, nc, V, codesT, codes_bf, wb_dram, smD)
    p.barrier()
    if mock_cc:
        for r_ in range(8):
            p.dma("gpsimd", wfull[r_ * 4096:(r_ + 1) * 4096, :], wb_dram,
                  "cc")
        p.wait("gpsimd", "cc", p.cnt["cc"])
    else:
        p.op("gpsimd", lambda e: e.collective_compute(
            "AllGather", ALU.bypass, replica_groups=[list(range(8))],
            ins=[wb_dram], outs=[wfull]), "cc")
        p.wait("gpsimd", "cc", 1)
    p.wait("scalar", "cc", p.cnt["cc"] if mock_cc else 1)
    for oc in range(DC):
        r = oc // 4
        csl = slice((oc % 4) * 128, (oc % 4) * 128 + 128)
        for qtr in range(4):
            src = wfull[r * 4096 + qtr * 1024:r * 4096 + (qtr + 1) * 1024,
                        csl].rearrange("(a q) m -> q a m", q=128)
            p.dma("scalar", wfin4[oc][:, qtr * 8:(qtr + 1) * 8, :], src, "rB")
    p.barrier()

    ph_load_x(p, V, x_fm, xbf)
    p.barrier()
    ph_proj(p, nc, V, wq_b, DC, NH, xch, q_scr, scale=INVSQ)
    p.barrier()
    ph_proj(p, nc, V, wk_b, DC, NH, xch, k_scr)
    p.barrier()
    ph_proj(p, nc, V, wv_b, DC, NH, xch, v_scr)
    p.barrier()

    p.dma("sync", Abf[:, 0:2048], cD_in, "rA")
    p.dma("sync", Abf[:, 2048:4096], sD_in, "rA")
    p.barrier()
    heads0 = [(h, q_scr[h], k_scr[h], v_scr[h], h, h, False)
              for h in range(NH)]
    ph_attention(p, nc, V, 0, heads0, attn_scr, sm)
    p.barrier()

    ph_load_A(p, Abf, [attn_scr[c] for c in range(DC)])
    p.barrier()
    ph_resid_ln(p, nc, V, wo_b, DC, xch, xbf, x1, False, smR,
                resid_scr=resid_scr)
    p.barrier()

    ph_load_A(p, Abf, [x1[c] for c in range(DC)])
    p.barrier()
    ph_proj(p, nc, V, u1_b, DC, 8, xch, p_scr)
    p.barrier()
    ph_swiglu(p, nc, V, v1_b, sm["b1s"], [p_scr[i] for i in range(8)],
              act_scr)
    p.barrier()
    ph_z(p, nc, V, u2_b, act_scr, z_scr)
    p.barrier()
    ph_load_A(p, Abf, [z_scr[c] for c in range(8)])
    p.dma("sync", sm["bias32"][:], b2T, "rA")
    p.barrier()
    ph_resid_ln(p, nc, V, v2_b, 8, xch, x1, x2, False, smR,
                bias_cols=sm["bias32"], resid_scr=resid_scr)
    p.barrier()

    ph_load_A(p, Abf, [x2[c] for c in range(DC)])
    p.barrier()
    ph_proj(p, nc, V, win_b, DC, 48, xch, qkv_scr)
    p.barrier()

    p.dma("sync", Abf[:, 0:2048], cD_in, "rA")
    p.dma("sync", Abf[:, 2048:4096], sD_in, "rA")
    p.dma("sync", Abf[:, 4096:6144], masks_in, "rA")
    p.barrier()
    heads1 = [(4 * hk + g, qkv_scr[4 * hk + g], qkv_scr[32 + hk],
               qkv_scr[40 + hk], 4 * hk + g, hk, True)
              for hk in range(8) for g in range(4)]
    ph_attention(p, nc, V, 1, heads1, attn_scr, sm)
    p.barrier()

    ph_load_A(p, Abf, [attn_scr[c] for c in range(DC)])
    p.barrier()
    ph_resid_ln(p, nc, V, wout_b, DC, xch, x2, x3, False, smR,
                resid_scr=resid_scr)
    p.barrier()

    ph_load_A(p, Abf, [x3[c] for c in range(DC)])
    p.dma("sync", sm["bias32"][:], bL1T, "rA")
    p.barrier()
    wfin_v = [wfin4[oc].rearrange("q a m -> q (a m)") for oc in range(DC)]
    ph_resid_ln(p, nc, V, wfin_v, DC, xch, None, y, True, smR,
                bias_cols=sm["bias32"], wbf16=True, resid_scr=resid_scr)
    p.barrier()
    p.emit()
    return nc, es


FUSED = True


def kernel_fused(inputs):
    sh, percore = _prep(**inputs)
    keys = ["wq_b", "wk_b", "wv_b", "wo_b", "u1_b", "v1_b", "u2_b", "v2_b",
            "win_b", "wout_b", "b1T", "b2T", "bL1T", "cbE", "iot_in",
            "cD_in", "sD_in", "ident_in", "masks_in"]
    in_maps = _maps(sh, percore, keys, ["x_fm", "codesT", "myscale"])
    res = _run(build_fused, in_maps)
    out = np.empty((8, S, D), np.float32)
    for r in range(8):
        out[r] = res[r]["y"].T
    return out


def _blk(W_io, OC):
    KIN = W_io.shape[0]
    KC = KIN // 128
    M = W_io.shape[1] // OC
    r = W_io.reshape(KC, 128, OC, M).transpose(2, 1, 0, 3)
    return np.ascontiguousarray(r.reshape(OC, 128, KC * M))


def _consts():
    inv = 1.0 / (10000.0 ** (np.arange(0, HD, 2) / HD))
    t = np.arange(S, dtype=np.float64)
    ang = np.outer(t, inv)
    c = np.cos(ang).astype(np.float32).T
    s = np.sin(ang).astype(np.float32).T
    cd = np.concatenate([c, c], 0).astype(BF)
    sd = np.concatenate([-s, s], 0).astype(BF)
    ident = np.eye(128, dtype=BF)
    masksv = np.zeros((128, 2048), np.float32)
    for mi, delta in enumerate([0, 128, 256, 384]):
        pp = np.arange(128)[:, None]
        ff = np.arange(512)[None, :]
        masksv[:, mi * 512:(mi + 1) * 512] = (pp + delta <= ff)
    iotv = np.stack([np.arange(128, dtype=np.float32),
                     np.arange(128, 256, dtype=np.float32)], 1)
    return cd, sd, ident, np.ascontiguousarray(masksv.astype(BF)), iotv


def _prep(L0_attn_x, L0_attn_Wq, L0_attn_Wk, L0_attn_Wv, L0_attn_Wo,
          L0_ffn_U1, L0_ffn_V1, L0_ffn_U2, L0_ffn_V2, L0_ffn_b1, L0_ffn_b2,
          L1_attn_w_in, L1_attn_w_out, L1_ffn_codes, L1_ffn_codebooks,
          L1_ffn_scales, L1_ffn_bias):
    f = np.asarray
    wq = f(L0_attn_Wq, np.float32)
    wk = f(L0_attn_Wk, np.float32)
    wv = f(L0_attn_Wv, np.float32)
    wo = f(L0_attn_Wo, np.float32)
    u1 = f(L0_ffn_U1, np.float32)
    v1 = f(L0_ffn_V1, np.float32)
    u2 = f(L0_ffn_U2, np.float32)
    v2 = f(L0_ffn_V2, np.float32)
    b1 = f(L0_ffn_b1, np.float32)
    b2 = f(L0_ffn_b2, np.float32)
    win = f(L1_attn_w_in, np.float32)
    wout = f(L1_attn_w_out, np.float32)
    codes = f(L1_ffn_codes, np.int32)
    cb = f(L1_ffn_codebooks, np.float32)
    scales = f(L1_ffn_scales, np.float32)
    bias = f(L1_ffn_bias, np.float32)
    x = f(L0_attn_x, np.float32)

    cd, sd, identv, masksv, iotv = _consts()
    sh = dict(
        wq_b=_blk(wq.T, DC), wk_b=_blk(wk.T, DC), wv_b=_blk(wv.T, DC),
        wo_b=_blk(wo.T, DC), u1_b=_blk(u1, 8), v1_b=_blk(v1, 172),
        u2_b=np.ascontiguousarray(
            u2.reshape(86, 128, 2, 512).transpose(2, 0, 1, 3)),
        v2_b=_blk(v2, DC), win_b=_blk(win.T, 48), wout_b=_blk(wout.T, DC),
        b1T=np.ascontiguousarray(b1.reshape(172, 128).T),
        b2T=np.ascontiguousarray(b2.reshape(32, 128).T),
        bL1T=np.ascontiguousarray(bias.reshape(32, 128).T),
        cbE=np.ascontiguousarray(cb.reshape(512, 8).reshape(4, 128, 8)
                                 .transpose(1, 0, 2).reshape(128, 32)),
        cD_in=cd, sD_in=sd, ident_in=identv, masks_in=masksv, iot_in=iotv,
    )
    percore = []
    for r in range(8):
        m = {}
        m["x_fm"] = np.ascontiguousarray(x[r].T)
        mc = codes[512 * r:512 * (r + 1)]
        m["codesT"] = np.ascontiguousarray(mc.transpose(1, 2, 0))
        m["myscale"] = np.ascontiguousarray(
            scales.reshape(-1)[512 * r:512 * (r + 1)].reshape(1, 512))
        percore.append(m)
    return sh, percore


def _run(builder, in_maps):
    nc, es = builder()
    res = run_bass_kernel_spmd(nc, in_maps, core_ids=list(range(8)))
    es.close()
    return res.results


def _maps(sh, percore, keys, pc_keys=()):
    out = []
    for r in range(8):
        m = {k: sh[k] for k in keys}
        for k in pc_keys:
            m[k] = percore[r][k]
        out.append(m)
    return out


def kernel(**inputs):
    if FUSED:
        return kernel_fused(inputs)
    sh, percore = _prep(**inputs)

    r1 = _run(build_qkv0,
              _maps(sh, percore, ["wq_b", "wk_b", "wv_b"], ["x_fm"]))
    r2 = _run(lambda: build_attn(0),
              [dict(cD_in=sh["cD_in"], sD_in=sh["sD_in"],
                    ident_in=sh["ident_in"], q_scr=r1[r]["q_scr"],
                    k_scr=r1[r]["k_scr"], v_scr=r1[r]["v_scr"])
               for r in range(8)])
    r3 = _run(lambda: build_resid_ln("wo_b", DC, "attn_scr", DC, "xbf",
                                     False),
              [dict(wo_b=sh["wo_b"], attn_scr=r2[r]["attn_scr"],
                    xbf=r1[r]["xbf"]) for r in range(8)])
    x1 = [r3[r]["xout"] for r in range(8)]
    r4 = _run(build_p0,
              [dict(x1=x1[r], u1_b=sh["u1_b"]) for r in range(8)])
    r5 = _run(build_swiglu,
              [dict(p_scr=r4[r]["p_scr"], v1_b=sh["v1_b"], b1T=sh["b1T"])
               for r in range(8)])
    r6 = _run(build_z,
              [dict(act_scr=r5[r]["act_scr"], u2_b=sh["u2_b"])
               for r in range(8)])
    r7 = _run(lambda: build_resid_ln("v2_b", 8, "z_scr", 8, "x1b",
                                     False, bias=True),
              [dict(v2_b=sh["v2_b"], z_scr=r6[r]["z_scr"], x1b=x1[r],
                    bias_t=sh["b2T"]) for r in range(8)])
    x2 = [r7[r]["xout"] for r in range(8)]
    r8 = _run(build_qkv1,
              [dict(x2=x2[r], win_b=sh["win_b"]) for r in range(8)])
    r9 = _run(lambda: build_attn(1),
              [dict(cD_in=sh["cD_in"], sD_in=sh["sD_in"],
                    ident_in=sh["ident_in"], masks_in=sh["masks_in"],
                    qkv_scr=r8[r]["qkv_scr"]) for r in range(8)])
    r10 = _run(lambda: build_resid_ln("wout_b", DC, "attn_scr", DC, "x2b",
                                      False),
               [dict(wout_b=sh["wout_b"], attn_scr=r9[r]["attn_scr"],
                     x2b=x2[r]) for r in range(8)])
    x3 = [r10[r]["xout"] for r in range(8)]
    r11 = _run(build_dequant,
               _maps(sh, percore, ["cbE", "iot_in"],
                     ["codesT", "myscale"]))
    WT = np.concatenate([r11[r]["wb_dram"] for r in range(8)], axis=1)
    wfin = _blk(WT, DC)
    r12 = _run(lambda: build_resid_ln("wfin", DC, "x3", DC, None,
                                      True, bias=True, wbf16=True),
               [dict(wfin=wfin, x3=x3[r], bias_t=sh["bL1T"])
                for r in range(8)])
    out = np.empty((8, S, D), np.float32)
    for r in range(8):
        out[r] = r12[r]["y"].T
    return out
